# version 18
# baseline (speedup 1.0000x reference)
import contextlib
import numpy as np
import ml_dtypes
import concourse.bass as bass
import concourse.mybir as mybir
from concourse.bass_utils import run_bass_kernel_spmd

F32 = mybir.dt.float32
BF16 = mybir.dt.bfloat16
AF = mybir.ActivationFunctionType
ALU = mybir.AluOpType
AX = mybir.AxisListType

T = 2048
TH = 1024
D = 1024
DI = 2048
NST = 16
NE = 18
ICH = 11
KC = 8
DC = 16
NTILE = 16
BIGNEG = -1.0e30

DBG = []
STOP = None


def _build():
    nc = bass.Bass("TRN2", target_bir_lowering=False, detect_race_conditions=False)

    def din(name, shape):
        return nc.dram_tensor(name, shape, F32, kind="ExternalInput").ap()

    xt_d = din("xt", [NTILE, 128, D])
    xf_d = din("xf", [KC, 128, T])
    wip_d = din("wip", [32, 128, KC, 128])
    wxp_d = din("wxp", [128, DC * 96])
    wdt_d = din("wdt", [64, DI])
    wop_d = din("wop", [DC, 128, D])
    wgt_d = din("wgt", [128, KC * 16])
    cgb_d = din("cgb", [128, 16])
    ceps_d = din("ceps", [128, 16])
    wcv_d = din("wcv", [128, DC * 4])
    cvb_d = din("cvb", [128, DC])
    cdb_d = din("cdb", [128, DC])
    cdp_d = din("cdp", [128, DC])
    cal_d = din("cal", [128, DC * NST])
    wg_d = din("wg3", [NE, ICH, 128, KC, 128])
    wu_d = din("wu3", [NE, ICH, 128, KC, 128])
    wd_d = din("wd3", [NE, ICH, 128, KC, 128])
    i128_d = din("i128f", [128, 128])
    selb_d = din("selb", [32, 32 * 128])
    of_d = nc.dram_tensor("of", [KC, 128, T], F32, kind="ExternalOutput").ap()

    ops = {e: [] for e in ("tensor", "vector", "scalar", "sync", "gpsimd")}
    cnt = {}
    sems = {}
    snapshots = {}

    def snap(name):
        snapshots[name] = {e: len(ops[e]) for e in ops}

    def mkinc(s, n=1):
        cnt[s] = cnt.get(s, 0) + n
        return cnt[s]

    def tgt(s):
        return cnt.get(s, 0)

    def EM(eng, sname=None, n=1):
        def deco(fn):
            if sname is None:
                ops[eng].append(fn)
            else:
                mkinc(sname, n)
                s, m = sname, n
                def wrapped(e, fn=fn):
                    fn(e).then_inc(sems[s], m)
                ops[eng].append(wrapped)
        return deco

    def W(eng, sname, val=None):
        v = tgt(sname) if val is None else val
        if v is None or v <= 0:
            return
        s = sname
        ops[eng].append(lambda e: e.wait_ge(sems[s], v))

    def Wg(eng, gate):
        if gate:
            W(eng, gate[0], gate[1])

    class DmaQ:
        def __init__(self, eng, lanes):
            self.eng, self.lanes = eng, lanes
            self.n = [0] * len(lanes)
            self.i = 0

        def dma(self, dst, src, gates=()):
            li = self.i % len(self.lanes)
            lane = self.lanes[li]
            if self.n[li] > 0:
                W(self.eng, lane, 16 * self.n[li])
            for g in gates:
                Wg(self.eng, g)
            @EM(self.eng, lane, 16)
            def _(e, dst=dst, src=src):
                return e.dma_start(dst, src)
            self.n[li] += 1
            self.i += 1
            return (lane, 16 * self.n[li])

        def barrier(self):
            return [(self.lanes[j], 16 * self.n[j])
                    for j in range(len(self.lanes)) if self.n[j] > 0]

    def VE1(fn):
        @EM("vector", "ve")
        def _(e, fn=fn):
            return fn(e)
        W("vector", "ve")

    def WT(eng, tok):
        if tok is None:
            return
        if isinstance(tok, tuple):
            tok = [tok]
        for (l, v) in tok:
            W(eng, l, v)

    with contextlib.ExitStack() as ctx:
        AR_F = 24800
        AR_B = 55296
        aF_t = ctx.enter_context(nc.sbuf_tensor("aF", [128, AR_F], F32))
        aB_t = ctx.enter_context(nc.sbuf_tensor("aB", [128, AR_B], BF16))
        pA = ctx.enter_context(nc.psum_tensor("pA", [128, 2048], F32))
        pB = ctx.enter_context(nc.psum_tensor("pB", [128, 2048], F32))
        for sname in ("wq0", "wq1", "wq2", "wq3", "hq0", "hq1", "st0",
                      "mm", "ve", "sc"):
            sems[sname] = ctx.enter_context(nc.semaphore(sname))

        def aF(off, n, p0=0, p1=128):
            assert off + n <= AR_F, (off, n)
            return aF_t[p0:p1, off:off + n]

        def aB(off, n, p0=0, p1=128):
            assert off + n <= AR_B, (off, n)
            return aB_t[p0:p1, off:off + n]

        def bankA(q, w=512):
            return pA[:, q * 512:q * 512 + w]

        def bankB(q, w=512):
            return pB[:, q * 512:q * 512 + w]

        wcv_s = aF(0, 64)
        cvb_s = aF(64, 16)
        cdb_s = aF(80, 16)
        cdp_s = aF(96, 16)
        a_s = aF(112, 256)
        wgt_s = aF(368, 128)
        gbias_s = aF(496, 16)
        epsi_s = aF(512, 16)
        c1_s = aF(528, 16)
        ones_all = aF(544, 128, 0, 65)
        ones32 = aF(544, 128, 32, 33)
        ones64 = aF(544, 128, 64, 65)
        cwf_s = aF(672, T, 0, 16)
        cwrow = aF(672, T, 32, 33)
        cwrow_b = aF(672, T, 64, 65)
        PF = 2720

        def bigF(off, n, p0=0, p1=128):
            return aF(PF + off, n, p0, p1)

        xdbl_s = bigF(0, T, 0, 96)
        xtok = [bigF(2048, D), bigF(3072, D)]
        xnt_f = [bigF(4096, D), bigF(5120, D)]
        xnf_blk = [bigF(6144, 128), bigF(6272, 128)]
        scores = bigF(6400, 256)
        tmpb = bigF(6656, 256)
        gsc = bigF(6912, 256)
        gsc2 = bigF(7168, 64)
        selm = bigF(7232, 256)
        cwt = bigF(7488, 256)
        scoreZ = bigF(7744, 512)
        scoreZ2 = bigF(8256, 512)
        mrow = bigF(8768, 16)
        vrow = bigF(8784, 16)
        rstd = bigF(8800, 16)
        nsc = bigF(8816, 16)
        nsc2 = bigF(8832, 16)
        i128f_s = bigF(8848, 128)
        wxp_s = bigF(2048, DC * 96)
        xpp1 = bigF(3584, T + 3)
        hh1 = bigF(5648, T)
        wdtb = [bigF(2048, 128, 0, 64), bigF(2176, 128, 0, 64)]
        state_s = bigF(2304, 256)
        ccar = bigF(2560, 48)
        xpp = bigF(2624, TH + 3)
        hh = bigF(3664, TH)
        res_s = bigF(4688, TH)
        delta = bigF(5712, TH)
        dh = bigF(6736, TH)
        atile = bigF(7760, TH)
        btile = bigF(8784, TH)
        yacc = bigF(9808, TH)
        us_full = bigF(10832, 1024)
        cwrep = bigF(11856, TH)
        outb = bigF(12880, KC * TH)
        assert 12880 + KC * TH <= AR_F - PF

        xnb = aB(0, KC * T)
        ybf = aB(16384, 3 * TH)
        actb = aB(20480, ICH * TH)
        rgw = [aB(31744 + i * 1024, KC * 128) for i in range(2)]
        ruw = [aB(33792 + i * 1024, KC * 128) for i in range(2)]
        rdw = [aB(35840 + i * 1024, KC * 128) for i in range(ICH)]
        rip = [aB(47104, KC * 128), aB(48128, KC * 128), aB(19456, KC * 128)]
        xdblb = aB(49152, T, 0, 32)
        selb = aB(51200, 32 * 128, 0, 32)
        xnt_b = [aB(31744, D), aB(32768, D)]
        i128b_s = aB(33792, 128)

        def v3(ap, a, b):
            return ap.rearrange("p (x y) -> p x y", y=b)

        xnb3 = v3(xnb, KC, T)
        ybf3 = v3(ybf, 3, TH)
        actb3 = v3(actb, ICH, TH)
        outb3 = v3(outb, KC, TH)
        selb3 = v3(selb, 32, 128)
        sc3 = v3(scores, NTILE, 16)
        tb3 = v3(tmpb, NTILE, 16)
        gs3 = v3(gsc, NTILE, 16)
        gs4 = gs3.rearrange("p i (g f) -> p i g f", f=4)
        g23 = v3(gsc2, NTILE, 4)
        se3 = v3(selm, NTILE, 16)
        cw3 = v3(cwt, NTILE, 16)

        HQ = DmaQ("sync", ["hq0", "hq1"])
        STQ = DmaQ("sync", ["st0"])
        GQ = DmaQ("gpsimd", ["wq0", "wq1", "wq2", "wq3"])

        def tap(name, view):
            if name in DBG:
                shp = [view.shape[0]] + list(view.shape[1:])
                dd = nc.dram_tensor("dbg_" + name, shp, view.dtype,
                                    kind="ExternalOutput").ap()
                tok = STQ.dma(dd, view, gates=[("ve", tgt("ve")), ("sc", tgt("sc")),
                                               ("mm", tgt("mm"))])
                for _eng in ("scalar", "vector", "tensor", "gpsimd"):
                    WT(_eng, tok)

        class Ring:
            def __init__(self, slots):
                self.slots = slots
                self.nload = 0
                self.nuse = 0
                self.srcs = []
                self.rel = []
                self.wqt = []

            def _pump(self):
                R = len(self.slots)
                while self.nload < len(self.srcs):
                    i = self.nload
                    if i >= R and i - R >= len(self.rel):
                        break
                    gates = [self.rel[i - R]] if i >= R else []
                    self.wqt.append(GQ.dma(self.slots[i % R], self.srcs[i],
                                           gates=gates))
                    self.nload += 1

            def acquire(self, src):
                i = self.nuse
                self.nuse += 1
                self.srcs.append(src)
                self._pump()
                assert i < self.nload, "ring underflow"
                return self.wqt[i], self.slots[i % len(self.slots)]

            def release(self):
                self.rel.append(("mm", tgt("mm")))
                self._pump()

        ring_ip = Ring(rip)
        ring_g = Ring(rgw)
        ring_u = Ring(ruw)
        ring_d = Ring(rdw)

        for dst, src in [(wcv_s, wcv_d), (cvb_s, cvb_d), (cdb_s, cdb_d),
                         (cdp_s, cdp_d), (a_s, cal_d), (wgt_s, wgt_d),
                         (gbias_s, cgb_d), (epsi_s, ceps_d), (i128f_s, i128_d)]:
            HQ.dma(dst, src)
        hq_consts = HQ.barrier()
        tok_selb = GQ.dma(selb, selb_d)

        WT("vector", hq_consts)
        @EM("vector")
        def _(e):
            e.memset(ones_all, 1.0)
            e.memset(scoreZ, 1.0)
            e.memset(c1_s, 1.0)
        @EM("vector", "ve")
        def _(e):
            return e.tensor_copy(i128b_s, i128f_s)
        WT("scalar", hq_consts)
        @EM("scalar", "sc")
        def _(e):
            return e.activation(a_s, a_s, AF.Exp)
        W("vector", "sc")
        @EM("vector", "ve")
        def _(e):
            return e.tensor_scalar_mul(a_s, a_s, -1.0)
        ve_prep = tgt("ve")

        ve_sq_done = {}
        for i in range(NTILE):
            gates = [("ve", ve_sq_done[i - 2])] if i >= 2 else []
            tok_x = HQ.dma(xtok[i % 2], xt_d[i], gates=gates)
            WT("scalar", tok_x)
            if i >= 2:
                W("scalar", "ve", ve_sq_done[i - 2])
            @EM("scalar", "sc")
            def _(e, i=i):
                return e.activation(xnt_f[i % 2], xtok[i % 2], AF.Square)
            W("vector", "sc")
            @EM("vector")
            def _(e, i=i):
                e.reduce_sum(mrow[:, i:i + 1], xtok[i % 2], AX.X)
            @EM("vector", "ve")
            def _(e, i=i):
                return e.reduce_sum(vrow[:, i:i + 1], xnt_f[i % 2], AX.X)
            ve_sq_done[i] = tgt("ve")
        W("vector", "ve")
        VE1(lambda e: e.tensor_scalar_mul(mrow, mrow, 1.0 / D))
        VE1(lambda e: e.tensor_scalar_mul(vrow, vrow, 1.0 / D))
        VE1(lambda e: e.tensor_tensor(nsc, mrow, mrow, ALU.mult))
        VE1(lambda e: e.tensor_tensor(vrow, vrow, nsc, ALU.subtract))
        VE1(lambda e: e.tensor_scalar_add(vrow, vrow, 1e-5))
        W("scalar", "ve")
        @EM("scalar", "sc")
        def _(e):
            return e.activation(rstd, vrow, AF.Sqrt)
        W("vector", "sc")
        VE1(lambda e: e.tensor_copy(scoreZ[:, 0:NTILE], rstd))
        VE1(lambda e: e.reciprocal(scoreZ2, scoreZ))
        VE1(lambda e: e.tensor_copy(rstd, scoreZ2[:, 0:NTILE]))
        VE1(lambda e: e.tensor_tensor(nsc, rstd, rstd, ALU.mult))
        VE1(lambda e: e.tensor_tensor(nsc, nsc, vrow, ALU.mult))
        VE1(lambda e: e.tensor_scalar(nsc, nsc, -0.5, 1.5, ALU.mult, ALU.add))
        VE1(lambda e: e.tensor_tensor(rstd, rstd, nsc, ALU.mult))
        ve_rstd = tgt("ve")

        sc_bank = {}
        ve_xn_gate = {}
        mm_tile_done = {}
        for i in range(NTILE):
            gates = [("ve", ve_xn_gate[i - 2])] if i >= 2 else [("ve", ve_rstd)]
            tok_x = HQ.dma(xtok[i % 2], xt_d[i], gates=gates)
            WT("vector", tok_x)
            if i == 0:
                W("vector", "ve", ve_rstd)
            if i >= 2:
                W("vector", "mm", mm_tile_done[i - 2])
            @EM("vector", "ve")
            def _(e, i=i):
                e.tensor_scalar(xnt_f[i % 2], xtok[i % 2],
                                mrow[:, i:i + 1], rstd[:, i:i + 1],
                                ALU.subtract, ALU.mult)
                return e.tensor_scalar(xnt_b[i % 2], xtok[i % 2],
                                       mrow[:, i:i + 1], rstd[:, i:i + 1],
                                       ALU.subtract, ALU.mult)
            ve_xn = tgt("ve")
            ve_xn_gate[i] = ve_xn
            W("tensor", "ve", ve_xn)
            if i == 0:
                WT("tensor", hq_consts)
                W("tensor", "ve", ve_prep)
            for c in range(KC):
                bk = ("bt", c % 2)
                W("tensor", "sc", sc_bank.get(bk, 0))
                @EM("tensor", "mm")
                def _(e, i=i, c=c):
                    return e.matmul(bankA(c % 2, 128),
                                    xnt_b[i % 2][:, c * 128:(c + 1) * 128],
                                    i128b_s, start=True, stop=True)
                W("scalar", "mm", tgt("mm"))
                @EM("scalar", "sc")
                def _(e, i=i, c=c):
                    return e.activation(xnb3[:, c, i * 128:(i + 1) * 128],
                                        bankA(c % 2, 128), AF.Copy)
                sc_bank[bk] = tgt("sc")
                bk = ("ft", c % 2)
                W("tensor", "sc", sc_bank.get(bk, 0))
                @EM("tensor", "mm")
                def _(e, i=i, c=c):
                    return e.matmul(bankA(2 + c % 2, 128),
                                    xnt_f[i % 2][:, c * 128:(c + 1) * 128],
                                    i128f_s, start=True, stop=True)
                W("scalar", "mm", tgt("mm"))
                @EM("scalar", "sc")
                def _(e, c=c):
                    return e.activation(xnf_blk[c % 2], bankA(2 + c % 2, 128), AF.Copy)
                sc_bank[bk] = tgt("sc")
                W("tensor", "sc", sc_bank[bk])
                if c == 0:
                    W("tensor", "sc", sc_bank.get(("gate", i % 2), 0))
                @EM("tensor", "mm")
                def _(e, i=i, c=c):
                    return e.matmul(pB[:, (i % 2) * 512:(i % 2) * 512 + 16],
                                    xnf_blk[c % 2], wgt_s[:, c * 16:(c + 1) * 16],
                                    start=(c == 0), stop=(c == KC - 1))
            mm_tile_done[i] = tgt("mm")
            W("scalar", "mm", mm_tile_done[i])
            @EM("scalar", "sc")
            def _(e, i=i):
                return e.activation(sc3[:, i, :],
                                    pB[:, (i % 2) * 512:(i % 2) * 512 + 16],
                                    AF.Sigmoid)
            sc_bank[("gate", i % 2)] = tgt("sc")
        sc_scores = tgt("sc")
        tap("xnb", xnb3)
        tap("scores", sc3)

        @EM("vector", "ve")
        def _(e):
            return e.memset(xpp1[:, 0:3], 0.0)
        ve_pad1 = tgt("ve")

        W("vector", "sc", sc_scores)
        t4 = tb3.rearrange("p i (g f) -> p i g f", f=4)
        VE1(lambda e: e.tensor_tensor(tb3, sc3,
            gbias_s[:, None, :].broadcast_to([128, NTILE, 16]), ALU.add))
        VE1(lambda e: e.tensor_tensor(gs4[:, :, :, 0], t4[:, :, :, 0], t4[:, :, :, 1], ALU.max))
        VE1(lambda e: e.tensor_tensor(gs4[:, :, :, 1], t4[:, :, :, 0], t4[:, :, :, 1], ALU.min))
        VE1(lambda e: e.tensor_tensor(gs4[:, :, :, 2], t4[:, :, :, 2], t4[:, :, :, 3], ALU.max))
        VE1(lambda e: e.tensor_tensor(gs4[:, :, :, 3], t4[:, :, :, 2], t4[:, :, :, 3], ALU.min))
        VE1(lambda e: e.tensor_tensor(g23, gs4[:, :, :, 0], gs4[:, :, :, 2], ALU.min))
        VE1(lambda e: e.tensor_tensor(gs4[:, :, :, 1], gs4[:, :, :, 1], gs4[:, :, :, 3], ALU.max))
        VE1(lambda e: e.tensor_tensor(g23, g23, gs4[:, :, :, 1], ALU.max))
        VE1(lambda e: e.tensor_tensor(gs4[:, :, :, 0], gs4[:, :, :, 0], gs4[:, :, :, 2], ALU.max))
        VE1(lambda e: e.tensor_tensor(g23, g23, gs4[:, :, :, 0], ALU.add))
        VE1(lambda e: e.tensor_tensor(gs3[:, :, 0:1], g23[:, :, 0:1], g23[:, :, 1:2], ALU.max))
        VE1(lambda e: e.tensor_tensor(gs3[:, :, 1:2], g23[:, :, 0:1], g23[:, :, 1:2], ALU.min))
        VE1(lambda e: e.tensor_tensor(gs3[:, :, 2:3], g23[:, :, 2:3], g23[:, :, 3:4], ALU.max))
        VE1(lambda e: e.tensor_tensor(gs3[:, :, 3:4], g23[:, :, 2:3], g23[:, :, 3:4], ALU.min))
        VE1(lambda e: e.tensor_tensor(nsc2[:, :, None], gs3[:, :, 0:1], gs3[:, :, 2:3], ALU.min))
        VE1(lambda e: e.tensor_tensor(nsc[:, :, None], gs3[:, :, 1:2], gs3[:, :, 3:4], ALU.max))
        VE1(lambda e: e.tensor_tensor(nsc[:, :, None], nsc[:, :, None], nsc2[:, :, None], ALU.max))
        VE1(lambda e: e.tensor_tensor(g23, g23,
            nsc[:, :, None].broadcast_to([128, NTILE, 4]), ALU.subtract))
        VE1(lambda e: e.tensor_single_scalar(g23, g23, 0.0, ALU.is_ge))
        VE1(lambda e: e.tensor_tensor(t4, t4,
            g23[:, :, :, None].broadcast_to([128, NTILE, 4, 4]), ALU.mult))
        VE1(lambda e: e.tensor_tensor(tb3, tb3,
            epsi_s[:, None, :].broadcast_to([128, NTILE, 16]), ALU.subtract))
        VE1(lambda e: e.memset(se3, 0.0))
        for r in range(4):
            VE1(lambda e: e.tensor_reduce(nsc, tb3, AX.X, ALU.max))
            VE1(lambda e: e.tensor_tensor(gs3, tb3,
                nsc[:, :, None].broadcast_to([128, NTILE, 16]), ALU.subtract))
            VE1(lambda e: e.tensor_single_scalar(gs3, gs3, 0.0, ALU.is_equal))
            VE1(lambda e: e.tensor_tensor(se3, se3, gs3, ALU.add))
            VE1(lambda e: e.scalar_tensor_tensor(tb3, gs3, BIGNEG, tb3, ALU.mult, ALU.add))
        VE1(lambda e: e.tensor_scalar_min(se3, se3, 1.0))
        VE1(lambda e: e.tensor_tensor(cw3, sc3, se3, ALU.mult))
        VE1(lambda e: e.tensor_reduce(nsc, cw3, AX.X, ALU.add))
        VE1(lambda e: e.memset(scoreZ, 1.0))
        VE1(lambda e: e.tensor_copy(scoreZ[:, 0:NTILE], nsc))
        VE1(lambda e: e.reciprocal(scoreZ2, scoreZ))
        VE1(lambda e: e.tensor_tensor(cw3, cw3,
            scoreZ2[:, 0:NTILE, None].broadcast_to([128, NTILE, 16]), ALU.mult))
        ve_cwt = tgt("ve")
        W("tensor", "ve", ve_cwt)
        for i in range(NTILE):
            bk = ("cw", i % 2)
            W("tensor", "sc", sc_bank.get(bk, 0))
            @EM("tensor", "mm")
            def _(e, i=i):
                return e.matmul(pB[0:16, (2 + i % 2) * 512:(2 + i % 2) * 512 + 128],
                                cw3[:, i, :], i128f_s, start=True, stop=True)
            W("scalar", "mm", tgt("mm"))
            @EM("scalar", "sc")
            def _(e, i=i):
                return e.activation(cwf_s[:, i * 128:(i + 1) * 128],
                                    pB[0:16, (2 + i % 2) * 512:(2 + i % 2) * 512 + 128],
                                    AF.Copy)
            sc_bank[bk] = tgt("sc")
        sc_cwf = tgt("sc")
        tap("cwf", cwf_s)
        snap("N")

        tok_wxp = HQ.dma(wxp_s, wxp_d, gates=[("ve", ve_xn_gate[NTILE - 1])])
        s1_entry = max(sc_bank[("bt", 0)], sc_bank[("bt", 1)],
                       sc_bank[("ft", 0)], sc_bank[("ft", 1)])
        W("tensor", "sc", s1_entry)
        WT("tensor", tok_wxp)
        sc_xp_ev = 0
        for k in range(DC):
            wv, wt = ring_ip.acquire(wip_d[k])
            WT("tensor", wv)
            W("tensor", "sc", sc_xp_ev)
            @EM("tensor", "mm")
            def _(e, wt=wt):
                w3 = v3(wt, KC, 128)
                for c in range(KC):
                    for tt in range(4):
                        ins = e.matmul(bankA(tt), w3[:, c, :],
                                       xnb3[:, c, tt * 512:(tt + 1) * 512],
                                       start=(c == 0), stop=(c == KC - 1))
                return ins
            ring_ip.release()
            mm_xp = tgt("mm")
            W("scalar", "mm", mm_xp)
            if k == 0:
                W("scalar", "ve", ve_pad1)
            for tt in range(4):
                @EM("scalar", "sc")
                def _(e, tt=tt):
                    return e.activation(xpp1[:, 3 + tt * 512: 3 + (tt + 1) * 512],
                                        bankA(tt), AF.Copy)
            sc_xp_ev = tgt("sc")
            W("vector", "sc", sc_xp_ev)
            if k >= 1:
                W("vector", "mm", mm_xproj_done)
            @EM("vector", "ve")
            def _(e, k=k):
                e.tensor_scalar_mul(hh1, xpp1[:, 0:T], wcv_s[:, k * 4:k * 4 + 1])
                e.scalar_tensor_tensor(hh1, xpp1[:, 1:1 + T], wcv_s[:, k * 4 + 1:k * 4 + 2],
                                       hh1, ALU.mult, ALU.add)
                e.scalar_tensor_tensor(hh1, xpp1[:, 2:2 + T], wcv_s[:, k * 4 + 2:k * 4 + 3],
                                       hh1, ALU.mult, ALU.add)
                return e.scalar_tensor_tensor(hh1, xpp1[:, 3:3 + T],
                                              wcv_s[:, k * 4 + 3:k * 4 + 4],
                                              hh1, ALU.mult, ALU.add)
            ve_conv = tgt("ve")
            W("scalar", "ve", ve_conv)
            @EM("scalar", "sc")
            def _(e, k=k):
                return e.activation(hh1, hh1, AF.Silu, bias=cvb_s[:, k:k + 1])
            sc_h = tgt("sc")
            W("tensor", "sc", sc_h)
            @EM("tensor", "mm")
            def _(e, k=k):
                for tt in range(4):
                    ins = e.matmul(bankB(tt)[0:96, :],
                                   v3(wxp_s, DC, 96)[:, k, :],
                                   hh1[:, tt * 512:(tt + 1) * 512],
                                   start=(k == 0), stop=(k == DC - 1))
                return ins
            mm_xproj_done = tgt("mm")
        W("scalar", "mm", mm_xproj_done)
        for tt in range(4):
            @EM("scalar", "sc")
            def _(e, tt=tt):
                return e.activation(xdbl_s[:, tt * 512:(tt + 1) * 512],
                                    bankB(tt)[0:96, :], AF.Copy)
        sc_xdbl = tgt("sc")
        W("vector", "sc", sc_xdbl)
        @EM("vector", "ve")
        def _(e):
            return e.tensor_copy(xdblb, xdbl_s[64:96, :])
        ve_xdblb = tgt("ve")
        tap("xdbl", xdbl_s)
        snap("S1")

        ve_half_store = None
        cw_state = {"mm": {0: 0, 1: 0}, "tok": {}}
        h0_store_toks = []
        pairs = [
            dict(full=pA[:, 0:1024], half=[bankA(0), bankA(1)], gate=("sc", sc_xdbl)),
            dict(full=pA[:, 1024:2048], half=[bankA(2), bankA(3)], gate=("sc", sc_xdbl)),
            dict(full=pB[:, 0:1024], half=[bankB(0), bankB(1)], gate=("sc", sc_xdbl)),
        ]
        pcur = [0]
        bankgate6 = [("sc", sc_xdbl)]
        pC = pairs[2]

        def pool_acquire():
            p = pairs[pcur[0] % 2]
            pcur[0] += 1
            Wg("tensor", p["gate"])
            return p

        for h in range(2):
            h0 = h * TH
            prevg = ("ve", ve_half_store) if ve_half_store else ("sc", sc_xdbl)
            gq_stage = []
            uq_ready = []
            silu_pend = []
            sc_silu_of = {}
            ve_act_of = {}
            uq_emitted = {}
            downs_done = {}
            for e_ in range(NE):
                for i in range(ICH):
                    gq_stage.append(("g", e_, i))
            down_queued = set()
            wdt_gate = [prevg] if h == 0 else [("mm", mm_dt_h0)]
            tok_wdt = [GQ.dma(wdtb[0], wdt_d[:, 0:128], gates=wdt_gate),
                       GQ.dma(wdtb[1], wdt_d[:, 128:256], gates=wdt_gate)]
            tok_xf = []
            for fc in range(KC):
                tok_xf.append(STQ.dma(outb3[:, fc, :], xf_d[fc][:, h0:h0 + TH],
                                      gates=[]))
            if h == 0:
                W("vector", "sc", sc_cwf)
                W("vector", "ve", ve_xdblb)
                @EM("vector", "ve")
                def _(e):
                    return e.memset(xpp[:, 0:3], 0.0)
            else:
                W("vector", "ve", ve_half_store)

            def emit_g(e_, i):
                if i == 0 and e_ < 16:
                    buf = e_ % 2
                    cw_state["tok"][e_] = HQ.dma(
                        cwrow if buf == 0 else cwrow_b, cwf_s[e_:e_ + 1, :],
                        gates=[("mm", cw_state["mm"][buf]), ("sc", sc_cwf)])
                gv, gw = ring_g.acquire(wg_d[e_, i])
                p = pool_acquire()
                WT("tensor", gv)
                @EM("tensor", "mm")
                def _(e, gw=gw, h0=h0, p=p):
                    w3 = v3(gw, KC, 128)
                    for c in range(KC):
                        for tt in range(2):
                            ins = e.matmul(p["half"][tt], w3[:, c, :],
                                           xnb3[:, c, h0 + tt * 512:h0 + (tt + 1) * 512],
                                           start=(c == 0), stop=(c == KC - 1))
                    return ins
                ring_g.release()
                mm_g = tgt("mm")
                W("scalar", "mm", mm_g)
                Wg("scalar", dn_actb_gate.get(i))
                @EM("scalar", "sc")
                def _(e, i=i, p=p):
                    return e.activation(actb3[:, i, :], p["full"], AF.Copy)
                p["gate"] = ("sc", tgt("sc"))
                def mksilu(e_=e_, i=i):
                    def em():
                        @EM("scalar", "sc")
                        def _(e):
                            return e.activation(actb3[:, i, :], actb3[:, i, :], AF.Silu)
                        sc_silu_of[(e_, i)] = tgt("sc")
                    return em
                silu_pend.append(mksilu())

            def emit_cwrep(e_):
                buf = e_ % 2
                row = cwrow if buf == 0 else cwrow_b
                one = ones32 if buf == 0 else ones64
                p = pool_acquire()
                WT("tensor", cw_state["tok"].pop(e_))
                @EM("tensor", "mm")
                def _(e, h0=h0, p=p, row=row, one=one):
                    for tt in range(2):
                        ins = e.matmul(p["half"][tt], one,
                                       row[:, h0 + tt * 512:h0 + (tt + 1) * 512],
                                       start=True, stop=True)
                    return ins
                cw_state["mm"][buf] = tgt("mm")
                W("vector", "mm", cw_state["mm"][buf])
                @EM("vector", "ve")
                def _(e, p=p):
                    return e.tensor_copy(cwrep, p["full"])
                p["gate"] = ("ve", tgt("ve"))

            def emit_u(e_, i):
                uv, uw = ring_u.acquire(wu_d[e_, i])
                p = pool_acquire()
                WT("tensor", uv)
                @EM("tensor", "mm")
                def _(e, uw=uw, h0=h0, p=p):
                    w3 = v3(uw, KC, 128)
                    for c in range(KC):
                        for tt in range(2):
                            ins = e.matmul(p["half"][tt], w3[:, c, :],
                                           xnb3[:, c, h0 + tt * 512:h0 + (tt + 1) * 512],
                                           start=(c == 0), stop=(c == KC - 1))
                    return ins
                ring_u.release()
                mm_u = tgt("mm")
                if i == 0:
                    for j in range(ICH):
                        ring_d.acquire(wd_d[e_, j])
                W("vector", "mm", mm_u)
                W("vector", "sc", sc_silu_of[(e_, i)])
                if e_ < 16:
                    @EM("vector", "ve")
                    def _(e, p=p):
                        return e.tensor_tensor(us_full, p["full"], cwrep, ALU.mult)
                    p["gate"] = ("ve", tgt("ve"))
                    @EM("vector", "ve")
                    def _(e, i=i):
                        return e.tensor_tensor(actb3[:, i, :], actb3[:, i, :],
                                               us_full, ALU.mult)
                else:
                    @EM("vector", "ve")
                    def _(e, i=i, p=p):
                        return e.tensor_tensor(actb3[:, i, :], actb3[:, i, :],
                                               p["full"], ALU.mult)
                    p["gate"] = ("ve", tgt("ve"))
                ve_act_of[(e_, i)] = tgt("ve")
                uq_emitted[e_] = uq_emitted.get(e_, 0) + 1
                if uq_emitted[e_] == ICH and e_ not in down_queued:
                    down_queued.add(e_)
                    for hc in range(KC):
                        uq_ready.append(("down", e_, hc))

            def emit_down(e_, hc):
                if hc == 0:
                    lane_max = {}
                    for i in range(ICH):
                        l, v = ring_d.wqt[(h * NE + e_) * ICH + i]
                        lane_max[l] = max(lane_max.get(l, 0), v)
                    WT("tensor", list(lane_max.items()))
                    W("tensor", "ve", max(ve_act_of[(e_, i)] for i in range(ICH)))
                p = pool_acquire()
                @EM("tensor", "mm")
                def _(e, e_=e_, hc=hc, p=p):
                    for i in range(ICH):
                        w3 = v3(rdw[i], KC, 128)
                        for tt in range(2):
                            ins = e.matmul(p["half"][tt], w3[:, hc, :],
                                           actb3[:, i, tt * 512:(tt + 1) * 512],
                                           start=(i == 0), stop=(i == ICH - 1))
                    return ins
                if hc == KC - 1:
                    for _ in range(ICH):
                        ring_d.release()
                    for i in range(ICH):
                        dn_actb_gate[i] = ("mm", tgt("mm"))
                mm_dn = tgt("mm")
                W("vector", "mm", mm_dn)
                if hc == 0 and e_ == 0:
                    for t_ in tok_xf:
                        WT("vector", t_)
                @EM("vector", "ve")
                def _(e, hc=hc, p=p):
                    return e.tensor_tensor(outb3[:, hc, :], p["full"],
                                           outb3[:, hc, :], ALU.add)
                p["gate"] = ("ve", tgt("ve"))
                downs_done[e_] = downs_done.get(e_, 0) + 1

            dn_actb_gate = {}
            op_wts = {}

            def emit_op_q(g2, fc):
                if fc == 0:
                    toks, wts = [], []
                    for kc in (2 * g2, 2 * g2 + 1):
                        wv, wt = ring_ip.acquire(wop_d[kc])
                        toks.append(wv)
                        wts.append(wt)
                    op_wts[g2] = wts
                    for tok in toks:
                        WT("tensor", tok)
                    W("tensor", "ve", ve_y3[2 * g2 + 1])
                wts = op_wts[g2]
                p = pool_acquire()
                @EM("tensor", "mm")
                def _(e, g2=g2, fc=fc, wts=wts, p=p):
                    for tt in range(2):
                        for j in range(2):
                            w3 = v3(wts[j], KC, 128)
                            ins = e.matmul(p["half"][tt], w3[:, fc, :],
                                           ybf3[:, (2 * g2 + j) % 3,
                                                tt * 512:(tt + 1) * 512],
                                           start=(j == 0), stop=(j == 1))
                    return ins
                mm_opt = tgt("mm")
                W("vector", "mm", mm_opt)
                if g2 == 0 and fc == 0:
                    for t_ in tok_xf:
                        WT("vector", t_)
                @EM("vector", "ve")
                def _(e, fc=fc, p=p):
                    return e.tensor_tensor(outb3[:, fc, :], p["full"],
                                           outb3[:, fc, :], ALU.add)
                p["gate"] = ("ve", tgt("ve"))
                if fc == KC - 1:
                    ring_ip.release()
                    ring_ip.release()
                    mm_op_of_slot[(2 * g2) % 3] = tgt("mm")
                    mm_op_of_slot[(2 * g2 + 1) % 3] = tgt("mm")

            def pop_quanta(maxq):
                npop = 0
                while npop < maxq:
                    if uq_ready:
                        qd = uq_ready.pop(0)
                    elif gq_stage:
                        e_ = gq_stage[0][1]
                        if e_ >= 1 and downs_done.get(e_ - 1, 0) < KC:
                            return npop
                        qd = gq_stage.pop(0)
                    else:
                        return npop
                    if qd[0] == "g":
                        emit_g(qd[1], qd[2])
                    elif qd[0] == "cwrep":
                        emit_cwrep(qd[1])
                        continue
                    elif qd[0] == "u":
                        emit_u(qd[1], qd[2])
                    elif qd[0] == "op":
                        emit_op_q(qd[1], qd[2])
                    else:
                        emit_down(qd[1], qd[2])
                    npop += 1
                return npop

            def boundary():
                units = []
                for em in silu_pend:
                    em()
                silu_pend.clear()
                for key in list(sc_silu_of.keys()):
                    if key not in uq_done_mark:
                        uq_done_mark.add(key)
                        units.append(key)
                units.sort(key=lambda x: (x[0], x[1]))
                for (e_, i) in units:
                    if e_ < 16 and e_ not in cw_queued:
                        cw_queued.add(e_)
                        uq_ready.append(("cwrep", e_))
                    uq_ready.append(("u", e_, i))

            uq_done_mark = set()
            cw_queued = set()

            ve_y3 = {}
            ve_conv_prev = 0
            ve_nloop_end = 0
            ve_b = 0
            ve_ym = 0
            ve_scan = 0
            sc_ev = {0: None, 1: None}
            mm_op_of_slot = {}
            for k in range(DC):
                while any(q[0] == "op" for q in uq_ready):
                    pop_quanta(1)
                wv, wt = ring_ip.acquire(wip_d[k])
                p = pool_acquire()
                WT("tensor", wv)
                @EM("tensor", "mm")
                def _(e, wt=wt, h0=h0, p=p):
                    w3 = v3(wt, KC, 128)
                    for c in range(KC):
                        for tt in range(2):
                            ins = e.matmul(p["half"][tt], w3[:, c, :],
                                           xnb3[:, c, h0 + tt * 512:h0 + (tt + 1) * 512],
                                           start=(c == 0), stop=(c == KC - 1))
                    return ins
                ring_ip.release()
                mm_xp = tgt("mm")
                W("scalar", "mm", mm_xp)
                if k >= 1:
                    W("scalar", "ve", ve_conv_prev)
                elif h == 1:
                    W("scalar", "ve", ve_half_conv_last)
                @EM("scalar", "sc")
                def _(e, p=p):
                    return e.activation(xpp[:, 3:3 + TH], p["full"], AF.Copy)
                p["gate"] = ("sc", tgt("sc"))
                sc_xp2 = tgt("sc")
                wv, wt = ring_ip.acquire(wip_d[DC + k])
                p = pool_acquire()
                WT("tensor", wv)
                @EM("tensor", "mm")
                def _(e, wt=wt, h0=h0, p=p):
                    w3 = v3(wt, KC, 128)
                    for c in range(KC):
                        for tt in range(2):
                            ins = e.matmul(p["half"][tt], w3[:, c, :],
                                           xnb3[:, c, h0 + tt * 512:h0 + (tt + 1) * 512],
                                           start=(c == 0), stop=(c == KC - 1))
                    return ins
                ring_ip.release()
                mm_res = tgt("mm")
                W("scalar", "mm", mm_res)
                if k >= 1:
                    W("scalar", "ve", ve_y3.get(k - 1, 0))
                elif h == 1:
                    W("scalar", "ve", ve_half_y3_last)
                @EM("scalar", "sc")
                def _(e, p=p):
                    return e.activation(res_s, p["full"], AF.Copy)
                p["gate"] = ("sc", tgt("sc"))
                WT("tensor", tok_wdt[k % 2])
                p = pool_acquire()
                @EM("tensor", "mm")
                def _(e, k=k, h0=h0, p=p):
                    for tt in range(2):
                        ins = e.matmul(p["half"][tt], wdtb[k % 2],
                                       xdbl_s[0:64, h0 + tt * 512:h0 + (tt + 1) * 512],
                                       start=True, stop=True)
                    return ins
                mm_dt = tgt("mm")
                if k + 1 < DC:
                    tok_wdt[(k + 1) % 2] = HQ.dma(
                        wdtb[(k + 1) % 2], wdt_d[:, (k + 1) * 128:(k + 2) * 128],
                        gates=[("mm", mm_dt)])
                W("scalar", "mm", mm_dt)
                if k >= 1:
                    W("scalar", "ve", ve_nloop_end)
                elif h == 1:
                    W("scalar", "ve", ve_half_nloop_last)
                @EM("scalar", "sc")
                def _(e, p=p):
                    return e.activation(btile, p["full"], AF.Copy)
                p["gate"] = ("sc", tgt("sc"))
                sc_dpre = tgt("sc")
                pop_quanta(3)

                W("vector", "sc", sc_xp2)
                if h == 1:
                    VE1(lambda e, k=k: e.tensor_copy(xpp[:, 0:3],
                                                     ccar[:, k * 3:k * 3 + 3]))
                @EM("vector", "ve")
                def _(e, k=k):
                    e.tensor_scalar_mul(hh, xpp[:, 0:TH], wcv_s[:, k * 4:k * 4 + 1])
                    e.scalar_tensor_tensor(hh, xpp[:, 1:1 + TH],
                                           wcv_s[:, k * 4 + 1:k * 4 + 2],
                                           hh, ALU.mult, ALU.add)
                    e.scalar_tensor_tensor(hh, xpp[:, 2:2 + TH],
                                           wcv_s[:, k * 4 + 2:k * 4 + 3],
                                           hh, ALU.mult, ALU.add)
                    return e.scalar_tensor_tensor(hh, xpp[:, 3:3 + TH],
                                                  wcv_s[:, k * 4 + 3:k * 4 + 4],
                                                  hh, ALU.mult, ALU.add)
                if h == 0:
                    @EM("vector", "ve")
                    def _(e, k=k):
                        return e.tensor_copy(ccar[:, k * 3:k * 3 + 3], xpp[:, TH:TH + 3])
                ve_conv_prev = tgt("ve")
                W("vector", "sc", sc_dpre)
                @EM("vector", "ve")
                def _(e, k=k):
                    return e.tensor_scalar(btile, btile, cdb_s[:, k:k + 1], 80.0,
                                           ALU.add, ALU.min)
                ve_clamp = tgt("ve")

                W("scalar", "ve", ve_conv_prev)
                @EM("scalar", "sc")
                def _(e, k=k):
                    return e.activation(hh, hh, AF.Silu, bias=cvb_s[:, k:k + 1])
                @EM("scalar", "sc")
                def _(e):
                    return e.activation(res_s[:, :], res_s[:, :], AF.Silu)
                boundary()
                sc_hres = tgt("sc")
                W("scalar", "ve", ve_clamp)
                @EM("scalar", "sc")
                def _(e):
                    return e.activation(btile, btile, AF.Exp)
                @EM("scalar", "sc")
                def _(e):
                    return e.activation(delta, btile, AF.Ln, bias=c1_s[:, 0:1])
                sc_delta = tgt("sc")
                W("vector", "sc", sc_delta)
                @EM("vector", "ve")
                def _(e):
                    return e.tensor_tensor(dh, delta, hh, ALU.mult)
                ve_dh = tgt("ve")

                if h == 0 and k == 0:
                    W("tensor", "ve", ve_xdblb)
                    WT("tensor", tok_selb)
                for n in range(NST):
                    Wg("tensor", bankgate6[0])
                    @EM("tensor", "mm")
                    def _(e, n=n, h0=h0):
                        for q in range(2):
                            ins = e.matmul(bankB(2 + q), selb3[:, n, :],
                                           xdblb[0:32, h0 + q * 512:h0 + (q + 1) * 512],
                                           start=True, stop=True)
                        return ins
                    mm_B = tgt("mm")
                    Wg("tensor", pC["gate"])
                    @EM("tensor", "mm")
                    def _(e, n=n, h0=h0):
                        for q in range(2):
                            ins = e.matmul(pC["half"][q], selb3[:, NST + n, :],
                                           xdblb[0:32, h0 + q * 512:h0 + (q + 1) * 512],
                                           start=True, stop=True)
                        return ins
                    mm_C = tgt("mm")
                    if n > 0:
                        W("scalar", "ve", ve_scan)
                    else:
                        W("scalar", "ve", ve_clamp)
                    @EM("scalar", "sc")
                    def _(e, k=k, n=n):
                        return e.activation(atile, delta, AF.Exp,
                                            scale=a_s[:, k * NST + n:k * NST + n + 1])
                    sc_a = tgt("sc")
                    W("vector", "mm", mm_B)
                    if n == 0:
                        W("vector", "ve", ve_dh)
                    @EM("vector", "ve")
                    def _(e):
                        return e.tensor_tensor(btile, dh, pB[:, 1024:2048], ALU.mult)
                    ve_b = tgt("ve")
                    bankgate6[0] = ("ve", ve_b)
                    pop_quanta(1)
                    W("vector", "sc", sc_a)
                    if h == 0:
                        @EM("vector", "ve")
                        def _(e):
                            return e.tensor_tensor_scan(btile, atile, btile, 0.0,
                                                        ALU.mult, ALU.add)
                        ve_scan = tgt("ve")
                        W("vector", "ve")
                        @EM("vector", "ve")
                        def _(e, k=k, n=n):
                            return e.tensor_copy(state_s[:, k * NST + n:k * NST + n + 1],
                                                 btile[:, TH - 1:TH])
                    else:
                        @EM("vector", "ve")
                        def _(e, k=k, n=n):
                            return e.tensor_tensor_scan(
                                btile, atile, btile,
                                state_s[:, k * NST + n:k * NST + n + 1],
                                ALU.mult, ALU.add)
                        ve_scan = tgt("ve")
                    W("vector", "mm", mm_C)
                    if n == 0:
                        @EM("vector", "ve")
                        def _(e):
                            return e.tensor_tensor(yacc, btile, pC["full"], ALU.mult)
                        pC["gate"] = ("ve", tgt("ve"))
                    else:
                        @EM("vector", "ve")
                        def _(e):
                            return e.tensor_tensor(btile, btile, pC["full"], ALU.mult)
                        pC["gate"] = ("ve", tgt("ve"))
                        @EM("vector", "ve")
                        def _(e):
                            return e.tensor_tensor(yacc, yacc, btile, ALU.add)
                    ve_ym = tgt("ve")
                    pop_quanta(1)
                ve_nloop_end = tgt("ve")
                W("vector", "sc", sc_hres)
                W("vector", "mm", mm_op_of_slot.get(k % 3, 0))
                @EM("vector", "ve")
                def _(e, k=k):
                    e.scalar_tensor_tensor(yacc, hh, cdp_s[:, k:k + 1], yacc,
                                           ALU.mult, ALU.add)
                    return e.tensor_tensor(ybf3[:, k % 3, :], yacc, res_s[:, :], ALU.mult)
                ve_y3[k] = tgt("ve")
                ve_nloop_end = ve_y3[k]
                if k % 2 == 1:
                    for fc in range(KC):
                        uq_ready.append(("op", k // 2, fc))
            guard = 0
            while gq_stage or uq_ready or silu_pend:
                guard += 1
                assert guard < 200, "feeder stuck"
                if not uq_ready and (silu_pend or gq_stage):
                    pop_quanta(50)
                    boundary()
                else:
                    pop_quanta(50)
            ve_all = tgt("ve")
            for fc in range(KC):
                t_ = STQ.dma(of_d[fc][:, h0:h0 + TH], outb3[:, fc, :],
                             gates=[("ve", ve_all)] if fc == 0 else [])
                if h == 0:
                    h0_store_toks.append(t_)
            ve_half_store = ve_all
            if h == 0:
                ve_half_conv_last = ve_conv_prev
                ve_half_y3_last = ve_y3[DC - 1]
                ve_half_nloop_last = ve_nloop_end
                mm_dt_h0 = mm_dt
            tap(f"outh{h}", outb3)
            snap(f"H{h}")
        for (l, v) in STQ.barrier():
            W("sync", l, v)

        if STOP is not None:
            cut = snapshots[STOP]
            for e in ops:
                ops[e] = ops[e][:cut[e]]

        with nc.Block() as blk:
            @blk.tensor
            def _(e):
                for fn in ops["tensor"]:
                    fn(e)
            @blk.vector
            def _(e):
                for fn in ops["vector"]:
                    fn(e)
            @blk.scalar
            def _(e):
                for fn in ops["scalar"]:
                    fn(e)
            @blk.sync
            def _(e):
                for fn in ops["sync"]:
                    fn(e)
            @blk.gpsimd
            def _(e):
                for fn in ops["gpsimd"]:
                    fn(e)

    return nc


def _host_prep(inputs):
    f32 = np.float32
    ip = np.ascontiguousarray(inputs["L0_attn_in_proj_w"], dtype=f32)
    wip = np.ascontiguousarray(
        ip.reshape(KC, 128, 32, 128).transpose(2, 1, 0, 3))
    wxp = np.ascontiguousarray(
        inputs["L0_attn_x_proj_w"].reshape(DC, 128, 96).transpose(1, 0, 2).reshape(128, DC * 96),
        dtype=f32)
    wdt = np.ascontiguousarray(inputs["L0_attn_dt_proj_w"], dtype=f32)
    wop = np.ascontiguousarray(inputs["L0_attn_out_proj_w"].reshape(DC, 128, D), dtype=f32)
    wgt = np.ascontiguousarray(
        inputs["L0_ffn_gate_w"].T.reshape(KC, 128, 16).transpose(1, 0, 2).reshape(128, KC * 16),
        dtype=f32)
    cgb = np.ascontiguousarray(np.broadcast_to(inputs["L0_ffn_gate_bias"], (128, 16)), dtype=f32)
    ceps = np.ascontiguousarray(
        np.broadcast_to(np.arange(16, dtype=f32) * 1e-12, (128, 16)))
    wcv = np.ascontiguousarray(
        inputs["L0_attn_conv_w"].reshape(DC, 128, 4).transpose(1, 0, 2).reshape(128, DC * 4),
        dtype=f32)
    cvb = np.ascontiguousarray(inputs["L0_attn_conv_b"].reshape(DC, 128).T, dtype=f32)
    cdb = np.ascontiguousarray(inputs["L0_attn_dt_proj_b"].reshape(DC, 128).T, dtype=f32)
    cdp = np.ascontiguousarray(inputs["L0_attn_D_param"].reshape(DC, 128).T, dtype=f32)
    cal = np.ascontiguousarray(
        inputs["L0_attn_A_log"].reshape(DC, 128, NST).transpose(1, 0, 2).reshape(128, DC * NST),
        dtype=f32)

    def expert_lhsT(w):
        return np.ascontiguousarray(
            w.reshape(w.shape[0], ICH, 128, KC, 128).transpose(0, 1, 4, 3, 2))

    gp = np.asarray(inputs["L0_ffn_gate_proj"], dtype=f32)
    up = np.asarray(inputs["L0_ffn_up_proj"], dtype=f32)
    sg = np.asarray(inputs["L0_ffn_shared_gate_w"], dtype=f32).reshape(2, 1408, D)
    su = np.asarray(inputs["L0_ffn_shared_up_w"], dtype=f32).reshape(2, 1408, D)
    wg3 = expert_lhsT(np.concatenate([gp, sg], axis=0))
    wu3 = expert_lhsT(np.concatenate([up, su], axis=0))
    dp = np.asarray(inputs["L0_ffn_down_proj"], dtype=f32)
    sd = np.asarray(inputs["L0_ffn_shared_down_w"], dtype=f32)
    dall = np.concatenate([dp, sd.reshape(D, 2, 1408).transpose(1, 0, 2)], axis=0)
    wd3 = np.ascontiguousarray(
        dall.reshape(NE, KC, 128, ICH, 128).transpose(0, 3, 4, 1, 2))
    i128f = np.eye(128, dtype=f32)
    selb = np.zeros((32, 32, 128), dtype=f32)
    for n in range(32):
        selb[n, n, :] = 1.0
    selb = selb.reshape(32, 32 * 128)
    return dict(wip=wip, wxp=wxp, wdt=wdt, wop=wop, wgt=wgt, cgb=cgb,
                ceps=ceps, wcv=wcv, cvb=cvb, cdb=cdb, cdp=cdp, cal=cal,
                wg3=wg3, wu3=wu3, wd3=wd3, i128f=i128f, selb=selb)


def kernel(**inputs):
    x = np.asarray(inputs["L0_attn_x"], dtype=np.float32)
    shared = _host_prep(inputs)
    nc = _build()
    in_maps = []
    for b in range(8):
        xb = np.ascontiguousarray(x[b])
        m = dict(shared)
        m["xt"] = np.ascontiguousarray(xb.reshape(NTILE, 128, D))
        m["xf"] = np.ascontiguousarray(xb.T.reshape(KC, 128, T))
        in_maps.append(m)
    res = run_bass_kernel_spmd(nc, in_maps, core_ids=list(range(8)))
    out = np.empty((8, T, D), dtype=np.float32)
    for b in range(8):
        of = np.asarray(res.results[b]["of"])
        out[b] = of.reshape(D, T).T
    return out


# version 19
# speedup vs baseline: 1.2077x; 1.2077x over previous
import contextlib
import numpy as np
import ml_dtypes
import concourse.bass as bass
import concourse.mybir as mybir
from concourse.bass_utils import run_bass_kernel_spmd

F32 = mybir.dt.float32
BF16 = mybir.dt.bfloat16
AF = mybir.ActivationFunctionType
ALU = mybir.AluOpType
AX = mybir.AxisListType

T = 2048
TH = 1024
D = 1024
DI = 2048
NST = 16
NE = 18
ICH = 11
KC = 8
DC = 16
NTILE = 16
BIGNEG = -1.0e30

DBG = []
STOP = None


def _build():
    nc = bass.Bass("TRN2", target_bir_lowering=False, detect_race_conditions=False)

    def din(name, shape):
        return nc.dram_tensor(name, shape, F32, kind="ExternalInput").ap()

    xt_d = din("xt", [NTILE, 128, D])
    xf_d = din("xf", [KC, 128, T])
    wip_d = din("wip", [32, 128, KC, 128])
    wxp_d = din("wxp", [128, DC * 96])
    wdt_d = din("wdt", [64, DI])
    wop_d = din("wop", [DC, 128, D])
    wgt_d = din("wgt", [128, KC * 16])
    cgb_d = din("cgb", [128, 16])
    ceps_d = din("ceps", [128, 16])
    wcv_d = din("wcv", [128, DC * 4])
    cvb_d = din("cvb", [128, DC])
    cdb_d = din("cdb", [128, DC])
    cdp_d = din("cdp", [128, DC])
    cal_d = din("cal", [128, DC * NST])
    wg_d = din("wg3", [NE, ICH, 128, KC, 128])
    wu_d = din("wu3", [NE, ICH, 128, KC, 128])
    wd_d = din("wd3", [NE, ICH, 128, KC, 128])
    i128_d = din("i128f", [128, 128])
    selb_d = din("selb", [32, 32 * 128])
    of_d = nc.dram_tensor("of", [KC, 128, T], F32, kind="ExternalOutput").ap()

    ops = {e: [] for e in ("tensor", "vector", "scalar", "sync", "gpsimd")}
    cnt = {}
    sems = {}
    snapshots = {}

    def snap(name):
        snapshots[name] = {e: len(ops[e]) for e in ops}

    def mkinc(s, n=1):
        cnt[s] = cnt.get(s, 0) + n
        return cnt[s]

    def tgt(s):
        return cnt.get(s, 0)

    def EM(eng, sname=None, n=1):
        def deco(fn):
            if sname is None:
                ops[eng].append(fn)
            else:
                mkinc(sname, n)
                s, m = sname, n
                def wrapped(e, fn=fn):
                    fn(e).then_inc(sems[s], m)
                ops[eng].append(wrapped)
        return deco

    def W(eng, sname, val=None):
        v = tgt(sname) if val is None else val
        if v is None or v <= 0:
            return
        s = sname
        ops[eng].append(lambda e: e.wait_ge(sems[s], v))

    def Wg(eng, gate):
        if gate:
            W(eng, gate[0], gate[1])

    class DmaQ:
        def __init__(self, eng, lanes):
            self.eng, self.lanes = eng, lanes
            self.n = [0] * len(lanes)
            self.i = 0

        def dma(self, dst, src, gates=()):
            li = self.i % len(self.lanes)
            lane = self.lanes[li]
            if self.n[li] > 0:
                W(self.eng, lane, 16 * self.n[li])
            for g in gates:
                Wg(self.eng, g)
            @EM(self.eng, lane, 16)
            def _(e, dst=dst, src=src):
                return e.dma_start(dst, src)
            self.n[li] += 1
            self.i += 1
            return (lane, 16 * self.n[li])

        def barrier(self):
            return [(self.lanes[j], 16 * self.n[j])
                    for j in range(len(self.lanes)) if self.n[j] > 0]

    def VE1(fn):
        @EM("vector", "ve")
        def _(e, fn=fn):
            return fn(e)
        W("vector", "ve")

    def WT(eng, tok):
        if tok is None:
            return
        if isinstance(tok, tuple):
            tok = [tok]
        for (l, v) in tok:
            W(eng, l, v)

    with contextlib.ExitStack() as ctx:
        AR_F = 23792
        AR_B = 57312
        aF_t = ctx.enter_context(nc.sbuf_tensor("aF", [128, AR_F], F32))
        aB_t = ctx.enter_context(nc.sbuf_tensor("aB", [128, AR_B], BF16))
        pA = ctx.enter_context(nc.psum_tensor("pA", [128, 2048], F32))
        pB = ctx.enter_context(nc.psum_tensor("pB", [128, 2048], F32))
        for sname in ("wq0", "wq1", "wq2", "wq3", "hq0", "hq1", "st0",
                      "mm", "ve", "sc"):
            sems[sname] = ctx.enter_context(nc.semaphore(sname))

        def aF(off, n, p0=0, p1=128):
            assert off + n <= AR_F, (off, n)
            return aF_t[p0:p1, off:off + n]

        def aB(off, n, p0=0, p1=128):
            assert off + n <= AR_B, (off, n)
            return aB_t[p0:p1, off:off + n]

        def bankA(q, w=512):
            return pA[:, q * 512:q * 512 + w]

        def bankB(q, w=512):
            return pB[:, q * 512:q * 512 + w]

        wcv_s = aF(0, 64)
        cvb_s = aF(64, 16)
        cdb_s = aF(80, 16)
        cdp_s = aF(96, 16)
        a_s = aF(112, 256)
        wgt_s = aF(368, 128)
        gbias_s = aF(496, 16)
        epsi_s = aF(512, 16)
        c1_s = aF(528, 16)
        ones_all = aF(544, 128, 0, 65)
        ones32 = aF(544, 128, 32, 33)
        ones64 = aF(544, 128, 64, 65)
        cwf_s = aF(672, T, 0, 16)
        cwrow = aF(672, T, 32, 33)
        cwrow_b = aF(672, T, 64, 65)
        PF = 2720

        def bigF(off, n, p0=0, p1=128):
            return aF(PF + off, n, p0, p1)

        xdbl_s = bigF(0, T, 0, 96)
        xtok = [bigF(2048, D), bigF(3072, D)]
        xnt_f = [bigF(4096, D), bigF(5120, D)]
        xnf_blk = [bigF(6144, 128), bigF(6272, 128)]
        scores = bigF(6400, 256)
        tmpb = bigF(6656, 256)
        gsc = bigF(6912, 256)
        gsc2 = bigF(7168, 64)
        selm = bigF(7232, 256)
        cwt = bigF(7488, 256)
        scoreZ = bigF(7744, 512)
        scoreZ2 = bigF(8256, 512)
        mrow = bigF(8768, 16)
        vrow = bigF(8784, 16)
        rstd = bigF(8800, 16)
        nsc = bigF(8816, 16)
        nsc2 = bigF(8832, 16)
        i128f_s = bigF(8848, 128)
        wxp_s = bigF(2048, DC * 96)
        xpp1 = bigF(3584, T + 3)
        hh1 = bigF(5648, T)
        wdtb = [bigF(2048, 128, 0, 64), bigF(2176, 128, 0, 64)]
        state_s = bigF(2304, 256)
        ccar = bigF(2560, 48)
        xpp = bigF(2624, TH + 3)
        hh = bigF(3664, TH)
        res_s = bigF(4688, TH)
        delta = bigF(5712, TH)
        dh = bigF(6736, TH)
        atile = bigF(7760, TH)
        btile = bigF(8784, TH)
        yacc = bigF(9808, TH)
        us_full = bigF(10832, 1024)
        cwrep = bigF(11856, TH)
        outb = bigF(12880, KC * TH)
        assert 12880 + KC * TH <= AR_F - PF

        xnb = aB(0, KC * T)
        ybf = aB(16384, 3 * TH)
        actb = aB(20480, ICH * TH)
        rgu = [aB(31744, KC * 128), aB(32768, KC * 128), aB(33792, KC * 128),
               aB(34816, KC * 128), aB(55296, KC * 128)]
        rdw = [aB(35840 + i * 1024, KC * 128) for i in range(ICH)]
        rip = [aB(47104, KC * 128), aB(48128, KC * 128), aB(19456, KC * 128)]
        xdblb = aB(49152, T, 0, 32)
        selb = aB(51200, 32 * 128, 0, 32)
        xnt_b = [aB(31744, D), aB(32768, D)]
        i128b_s = aB(33792, 128)

        def v3(ap, a, b):
            return ap.rearrange("p (x y) -> p x y", y=b)

        xnb3 = v3(xnb, KC, T)
        ybf3 = v3(ybf, 3, TH)
        actb3 = v3(actb, ICH, TH)
        outb3 = v3(outb, KC, TH)
        selb3 = v3(selb, 32, 128)
        sc3 = v3(scores, NTILE, 16)
        tb3 = v3(tmpb, NTILE, 16)
        gs3 = v3(gsc, NTILE, 16)
        gs4 = gs3.rearrange("p i (g f) -> p i g f", f=4)
        g23 = v3(gsc2, NTILE, 4)
        se3 = v3(selm, NTILE, 16)
        cw3 = v3(cwt, NTILE, 16)

        HQ = DmaQ("sync", ["hq0", "hq1"])
        STQ = DmaQ("sync", ["st0"])
        GQ = DmaQ("gpsimd", ["wq0", "wq1", "wq2", "wq3"])

        def tap(name, view):
            if name in DBG:
                shp = [view.shape[0]] + list(view.shape[1:])
                dd = nc.dram_tensor("dbg_" + name, shp, view.dtype,
                                    kind="ExternalOutput").ap()
                tok = STQ.dma(dd, view, gates=[("ve", tgt("ve")), ("sc", tgt("sc")),
                                               ("mm", tgt("mm"))])
                for _eng in ("scalar", "vector", "tensor", "gpsimd"):
                    WT(_eng, tok)

        class Ring:
            def __init__(self, slots):
                self.slots = slots
                self.nload = 0
                self.nuse = 0
                self.srcs = []
                self.rel = []
                self.wqt = []

            def _pump(self):
                R = len(self.slots)
                while self.nload < len(self.srcs):
                    i = self.nload
                    if i >= R and i - R >= len(self.rel):
                        break
                    gates = [self.rel[i - R]] if i >= R else []
                    self.wqt.append(GQ.dma(self.slots[i % R], self.srcs[i],
                                           gates=gates))
                    self.nload += 1

            def acquire(self, src):
                i = self.nuse
                self.nuse += 1
                self.srcs.append(src)
                self._pump()
                assert i < self.nload, "ring underflow"
                return self.wqt[i], self.slots[i % len(self.slots)]

            def release(self):
                self.rel.append(("mm", tgt("mm")))
                self._pump()

        ring_ip = Ring(rip)
        ring_gu = Ring(rgu)
        ring_d = Ring(rdw)

        for dst, src in [(wcv_s, wcv_d), (cvb_s, cvb_d), (cdb_s, cdb_d),
                         (cdp_s, cdp_d), (a_s, cal_d), (wgt_s, wgt_d),
                         (gbias_s, cgb_d), (epsi_s, ceps_d), (i128f_s, i128_d)]:
            HQ.dma(dst, src)
        hq_consts = HQ.barrier()
        tok_selb = GQ.dma(selb, selb_d)

        WT("vector", hq_consts)
        @EM("vector")
        def _(e):
            e.memset(ones_all, 1.0)
            e.memset(scoreZ, 1.0)
            e.memset(c1_s, 1.0)
        @EM("vector", "ve")
        def _(e):
            return e.tensor_copy(i128b_s, i128f_s)
        WT("scalar", hq_consts)
        @EM("scalar", "sc")
        def _(e):
            return e.activation(a_s, a_s, AF.Exp)
        W("vector", "sc")
        @EM("vector", "ve")
        def _(e):
            return e.tensor_scalar_mul(a_s, a_s, -1.0)
        ve_prep = tgt("ve")

        ve_sq_done = {}
        for i in range(NTILE):
            gates = [("ve", ve_sq_done[i - 2])] if i >= 2 else []
            tok_x = HQ.dma(xtok[i % 2], xt_d[i], gates=gates)
            WT("scalar", tok_x)
            if i >= 2:
                W("scalar", "ve", ve_sq_done[i - 2])
            @EM("scalar", "sc")
            def _(e, i=i):
                return e.activation(xnt_f[i % 2], xtok[i % 2], AF.Square)
            W("vector", "sc")
            @EM("vector")
            def _(e, i=i):
                e.reduce_sum(mrow[:, i:i + 1], xtok[i % 2], AX.X)
            @EM("vector", "ve")
            def _(e, i=i):
                return e.reduce_sum(vrow[:, i:i + 1], xnt_f[i % 2], AX.X)
            ve_sq_done[i] = tgt("ve")
        W("vector", "ve")
        VE1(lambda e: e.tensor_scalar_mul(mrow, mrow, 1.0 / D))
        VE1(lambda e: e.tensor_scalar_mul(vrow, vrow, 1.0 / D))
        VE1(lambda e: e.tensor_tensor(nsc, mrow, mrow, ALU.mult))
        VE1(lambda e: e.tensor_tensor(vrow, vrow, nsc, ALU.subtract))
        VE1(lambda e: e.tensor_scalar_add(vrow, vrow, 1e-5))
        W("scalar", "ve")
        @EM("scalar", "sc")
        def _(e):
            return e.activation(rstd, vrow, AF.Sqrt)
        W("vector", "sc")
        VE1(lambda e: e.tensor_copy(scoreZ[:, 0:NTILE], rstd))
        VE1(lambda e: e.reciprocal(scoreZ2, scoreZ))
        VE1(lambda e: e.tensor_copy(rstd, scoreZ2[:, 0:NTILE]))
        VE1(lambda e: e.tensor_tensor(nsc, rstd, rstd, ALU.mult))
        VE1(lambda e: e.tensor_tensor(nsc, nsc, vrow, ALU.mult))
        VE1(lambda e: e.tensor_scalar(nsc, nsc, -0.5, 1.5, ALU.mult, ALU.add))
        VE1(lambda e: e.tensor_tensor(rstd, rstd, nsc, ALU.mult))
        ve_rstd = tgt("ve")

        sc_bank = {}
        ve_xn_gate = {}
        mm_tile_done = {}
        for i in range(NTILE):
            gates = [("ve", ve_xn_gate[i - 2])] if i >= 2 else [("ve", ve_rstd)]
            tok_x = HQ.dma(xtok[i % 2], xt_d[i], gates=gates)
            WT("vector", tok_x)
            if i == 0:
                W("vector", "ve", ve_rstd)
            if i >= 2:
                W("vector", "mm", mm_tile_done[i - 2])
            @EM("vector", "ve")
            def _(e, i=i):
                e.tensor_scalar(xnt_f[i % 2], xtok[i % 2],
                                mrow[:, i:i + 1], rstd[:, i:i + 1],
                                ALU.subtract, ALU.mult)
                return e.tensor_scalar(xnt_b[i % 2], xtok[i % 2],
                                       mrow[:, i:i + 1], rstd[:, i:i + 1],
                                       ALU.subtract, ALU.mult)
            ve_xn = tgt("ve")
            ve_xn_gate[i] = ve_xn
            W("tensor", "ve", ve_xn)
            if i == 0:
                WT("tensor", hq_consts)
                W("tensor", "ve", ve_prep)
            for c in range(KC):
                bk = ("bt", c % 2)
                W("tensor", "sc", sc_bank.get(bk, 0))
                @EM("tensor", "mm")
                def _(e, i=i, c=c):
                    return e.matmul(bankA(c % 2, 128),
                                    xnt_b[i % 2][:, c * 128:(c + 1) * 128],
                                    i128b_s, start=True, stop=True)
                W("scalar", "mm", tgt("mm"))
                @EM("scalar", "sc")
                def _(e, i=i, c=c):
                    return e.activation(xnb3[:, c, i * 128:(i + 1) * 128],
                                        bankA(c % 2, 128), AF.Copy)
                sc_bank[bk] = tgt("sc")
                bk = ("ft", c % 2)
                W("tensor", "sc", sc_bank.get(bk, 0))
                @EM("tensor", "mm")
                def _(e, i=i, c=c):
                    return e.matmul(bankA(2 + c % 2, 128),
                                    xnt_f[i % 2][:, c * 128:(c + 1) * 128],
                                    i128f_s, start=True, stop=True)
                W("scalar", "mm", tgt("mm"))
                @EM("scalar", "sc")
                def _(e, c=c):
                    return e.activation(xnf_blk[c % 2], bankA(2 + c % 2, 128), AF.Copy)
                sc_bank[bk] = tgt("sc")
                W("tensor", "sc", sc_bank[bk])
                if c == 0:
                    W("tensor", "sc", sc_bank.get(("gate", i % 2), 0))
                @EM("tensor", "mm")
                def _(e, i=i, c=c):
                    return e.matmul(pB[:, (i % 2) * 512:(i % 2) * 512 + 16],
                                    xnf_blk[c % 2], wgt_s[:, c * 16:(c + 1) * 16],
                                    start=(c == 0), stop=(c == KC - 1))
            mm_tile_done[i] = tgt("mm")
            W("scalar", "mm", mm_tile_done[i])
            @EM("scalar", "sc")
            def _(e, i=i):
                return e.activation(sc3[:, i, :],
                                    pB[:, (i % 2) * 512:(i % 2) * 512 + 16],
                                    AF.Sigmoid)
            sc_bank[("gate", i % 2)] = tgt("sc")
        sc_scores = tgt("sc")
        tap("xnb", xnb3)
        tap("scores", sc3)

        @EM("vector", "ve")
        def _(e):
            return e.memset(xpp1[:, 0:3], 0.0)
        ve_pad1 = tgt("ve")

        W("vector", "sc", sc_scores)
        t4 = tb3.rearrange("p i (g f) -> p i g f", f=4)
        VE1(lambda e: e.tensor_tensor(tb3, sc3,
            gbias_s[:, None, :].broadcast_to([128, NTILE, 16]), ALU.add))
        VE1(lambda e: e.tensor_tensor(gs4[:, :, :, 0], t4[:, :, :, 0], t4[:, :, :, 1], ALU.max))
        VE1(lambda e: e.tensor_tensor(gs4[:, :, :, 1], t4[:, :, :, 0], t4[:, :, :, 1], ALU.min))
        VE1(lambda e: e.tensor_tensor(gs4[:, :, :, 2], t4[:, :, :, 2], t4[:, :, :, 3], ALU.max))
        VE1(lambda e: e.tensor_tensor(gs4[:, :, :, 3], t4[:, :, :, 2], t4[:, :, :, 3], ALU.min))
        VE1(lambda e: e.tensor_tensor(g23, gs4[:, :, :, 0], gs4[:, :, :, 2], ALU.min))
        VE1(lambda e: e.tensor_tensor(gs4[:, :, :, 1], gs4[:, :, :, 1], gs4[:, :, :, 3], ALU.max))
        VE1(lambda e: e.tensor_tensor(g23, g23, gs4[:, :, :, 1], ALU.max))
        VE1(lambda e: e.tensor_tensor(gs4[:, :, :, 0], gs4[:, :, :, 0], gs4[:, :, :, 2], ALU.max))
        VE1(lambda e: e.tensor_tensor(g23, g23, gs4[:, :, :, 0], ALU.add))
        VE1(lambda e: e.tensor_tensor(gs3[:, :, 0:1], g23[:, :, 0:1], g23[:, :, 1:2], ALU.max))
        VE1(lambda e: e.tensor_tensor(gs3[:, :, 1:2], g23[:, :, 0:1], g23[:, :, 1:2], ALU.min))
        VE1(lambda e: e.tensor_tensor(gs3[:, :, 2:3], g23[:, :, 2:3], g23[:, :, 3:4], ALU.max))
        VE1(lambda e: e.tensor_tensor(gs3[:, :, 3:4], g23[:, :, 2:3], g23[:, :, 3:4], ALU.min))
        VE1(lambda e: e.tensor_tensor(nsc2[:, :, None], gs3[:, :, 0:1], gs3[:, :, 2:3], ALU.min))
        VE1(lambda e: e.tensor_tensor(nsc[:, :, None], gs3[:, :, 1:2], gs3[:, :, 3:4], ALU.max))
        VE1(lambda e: e.tensor_tensor(nsc[:, :, None], nsc[:, :, None], nsc2[:, :, None], ALU.max))
        VE1(lambda e: e.tensor_tensor(g23, g23,
            nsc[:, :, None].broadcast_to([128, NTILE, 4]), ALU.subtract))
        VE1(lambda e: e.tensor_single_scalar(g23, g23, 0.0, ALU.is_ge))
        VE1(lambda e: e.tensor_tensor(t4, t4,
            g23[:, :, :, None].broadcast_to([128, NTILE, 4, 4]), ALU.mult))
        VE1(lambda e: e.tensor_tensor(tb3, tb3,
            epsi_s[:, None, :].broadcast_to([128, NTILE, 16]), ALU.subtract))
        VE1(lambda e: e.memset(se3, 0.0))
        for r in range(4):
            VE1(lambda e: e.tensor_reduce(nsc, tb3, AX.X, ALU.max))
            VE1(lambda e: e.tensor_tensor(gs3, tb3,
                nsc[:, :, None].broadcast_to([128, NTILE, 16]), ALU.subtract))
            VE1(lambda e: e.tensor_single_scalar(gs3, gs3, 0.0, ALU.is_equal))
            VE1(lambda e: e.tensor_tensor(se3, se3, gs3, ALU.add))
            VE1(lambda e: e.scalar_tensor_tensor(tb3, gs3, BIGNEG, tb3, ALU.mult, ALU.add))
        VE1(lambda e: e.tensor_scalar_min(se3, se3, 1.0))
        VE1(lambda e: e.tensor_tensor(cw3, sc3, se3, ALU.mult))
        VE1(lambda e: e.tensor_reduce(nsc, cw3, AX.X, ALU.add))
        VE1(lambda e: e.memset(scoreZ, 1.0))
        VE1(lambda e: e.tensor_copy(scoreZ[:, 0:NTILE], nsc))
        VE1(lambda e: e.reciprocal(scoreZ2, scoreZ))
        VE1(lambda e: e.tensor_tensor(cw3, cw3,
            scoreZ2[:, 0:NTILE, None].broadcast_to([128, NTILE, 16]), ALU.mult))
        ve_cwt = tgt("ve")
        W("tensor", "ve", ve_cwt)
        for i in range(NTILE):
            bk = ("cw", i % 2)
            W("tensor", "sc", sc_bank.get(bk, 0))
            @EM("tensor", "mm")
            def _(e, i=i):
                return e.matmul(pB[0:16, (2 + i % 2) * 512:(2 + i % 2) * 512 + 128],
                                cw3[:, i, :], i128f_s, start=True, stop=True)
            W("scalar", "mm", tgt("mm"))
            @EM("scalar", "sc")
            def _(e, i=i):
                return e.activation(cwf_s[:, i * 128:(i + 1) * 128],
                                    pB[0:16, (2 + i % 2) * 512:(2 + i % 2) * 512 + 128],
                                    AF.Copy)
            sc_bank[bk] = tgt("sc")
        sc_cwf = tgt("sc")
        tap("cwf", cwf_s)
        snap("N")

        tok_wxp = HQ.dma(wxp_s, wxp_d, gates=[("ve", ve_xn_gate[NTILE - 1])])
        s1_entry = max(sc_bank[("bt", 0)], sc_bank[("bt", 1)],
                       sc_bank[("ft", 0)], sc_bank[("ft", 1)])
        W("tensor", "sc", s1_entry)
        WT("tensor", tok_wxp)
        sc_xp_ev = 0
        for k in range(DC):
            wv, wt = ring_ip.acquire(wip_d[k])
            WT("tensor", wv)
            W("tensor", "sc", sc_xp_ev)
            @EM("tensor", "mm")
            def _(e, wt=wt):
                w3 = v3(wt, KC, 128)
                for c in range(KC):
                    for tt in range(4):
                        ins = e.matmul(bankA(tt), w3[:, c, :],
                                       xnb3[:, c, tt * 512:(tt + 1) * 512],
                                       start=(c == 0), stop=(c == KC - 1))
                return ins
            ring_ip.release()
            mm_xp = tgt("mm")
            W("scalar", "mm", mm_xp)
            if k == 0:
                W("scalar", "ve", ve_pad1)
            for tt in range(4):
                @EM("scalar", "sc")
                def _(e, tt=tt):
                    return e.activation(xpp1[:, 3 + tt * 512: 3 + (tt + 1) * 512],
                                        bankA(tt), AF.Copy)
            sc_xp_ev = tgt("sc")
            W("vector", "sc", sc_xp_ev)
            if k >= 1:
                W("vector", "mm", mm_xproj_done)
            @EM("vector", "ve")
            def _(e, k=k):
                e.tensor_scalar_mul(hh1, xpp1[:, 0:T], wcv_s[:, k * 4:k * 4 + 1])
                e.scalar_tensor_tensor(hh1, xpp1[:, 1:1 + T], wcv_s[:, k * 4 + 1:k * 4 + 2],
                                       hh1, ALU.mult, ALU.add)
                e.scalar_tensor_tensor(hh1, xpp1[:, 2:2 + T], wcv_s[:, k * 4 + 2:k * 4 + 3],
                                       hh1, ALU.mult, ALU.add)
                return e.scalar_tensor_tensor(hh1, xpp1[:, 3:3 + T],
                                              wcv_s[:, k * 4 + 3:k * 4 + 4],
                                              hh1, ALU.mult, ALU.add)
            ve_conv = tgt("ve")
            W("scalar", "ve", ve_conv)
            @EM("scalar", "sc")
            def _(e, k=k):
                return e.activation(hh1, hh1, AF.Silu, bias=cvb_s[:, k:k + 1])
            sc_h = tgt("sc")
            W("tensor", "sc", sc_h)
            @EM("tensor", "mm")
            def _(e, k=k):
                for tt in range(4):
                    ins = e.matmul(bankB(tt)[0:96, :],
                                   v3(wxp_s, DC, 96)[:, k, :],
                                   hh1[:, tt * 512:(tt + 1) * 512],
                                   start=(k == 0), stop=(k == DC - 1))
                return ins
            mm_xproj_done = tgt("mm")
        W("scalar", "mm", mm_xproj_done)
        for tt in range(4):
            @EM("scalar", "sc")
            def _(e, tt=tt):
                return e.activation(xdbl_s[:, tt * 512:(tt + 1) * 512],
                                    bankB(tt)[0:96, :], AF.Copy)
        sc_xdbl = tgt("sc")
        W("vector", "sc", sc_xdbl)
        @EM("vector", "ve")
        def _(e):
            return e.tensor_copy(xdblb, xdbl_s[64:96, :])
        ve_xdblb = tgt("ve")
        tap("xdbl", xdbl_s)
        snap("S1")

        ve_half_store = None
        cw_state = {"mm": {0: 0, 1: 0}, "tok": {}}
        h0_store_toks = []
        pairs = [
            dict(full=pA[:, 0:1024], half=[bankA(0), bankA(1)], gate=("sc", sc_xdbl)),
            dict(full=pA[:, 1024:2048], half=[bankA(2), bankA(3)], gate=("sc", sc_xdbl)),
            dict(full=pB[:, 0:1024], half=[bankB(0), bankB(1)], gate=("sc", sc_xdbl)),
        ]
        pcur = [0]
        bankgate6 = [("sc", sc_xdbl)]
        pC = pairs[2]

        def pool_acquire():
            p = pairs[pcur[0] % 2]
            pcur[0] += 1
            Wg("tensor", p["gate"])
            return p

        for h in range(2):
            h0 = h * TH
            prevg = ("ve", ve_half_store) if ve_half_store else ("sc", sc_xdbl)
            gq_stage = []
            uq_ready = []
            silu_pend = []
            sc_silu_of = {}
            ve_act_of = {}
            uq_emitted = {}
            downs_done = {}
            for e_ in range(NE):
                for i in range(ICH):
                    gq_stage.append(("g", e_, i))
            down_queued = set()
            wdt_gate = [prevg] if h == 0 else [("mm", mm_dt_h0)]
            tok_wdt = [HQ.dma(wdtb[0], wdt_d[:, 0:128], gates=wdt_gate),
                       HQ.dma(wdtb[1], wdt_d[:, 128:256], gates=wdt_gate)]
            tok_xf = []
            for fc in range(KC):
                tok_xf.append(STQ.dma(outb3[:, fc, :], xf_d[fc][:, h0:h0 + TH],
                                      gates=[]))
            if h == 0:
                W("vector", "sc", sc_cwf)
                W("vector", "ve", ve_xdblb)
                @EM("vector", "ve")
                def _(e):
                    return e.memset(xpp[:, 0:3], 0.0)
            else:
                W("vector", "ve", ve_half_store)

            def emit_g(e_, i):
                if i == 0 and e_ < 16:
                    buf = e_ % 2
                    cw_state["tok"][e_] = HQ.dma(
                        cwrow if buf == 0 else cwrow_b, cwf_s[e_:e_ + 1, :],
                        gates=[("mm", cw_state["mm"][buf]), ("sc", sc_cwf)])
                gv, gw = ring_gu.acquire(wg_d[e_, i])
                p = pool_acquire()
                WT("tensor", gv)
                @EM("tensor", "mm")
                def _(e, gw=gw, h0=h0, p=p):
                    w3 = v3(gw, KC, 128)
                    for c in range(KC):
                        for tt in range(2):
                            ins = e.matmul(p["half"][tt], w3[:, c, :],
                                           xnb3[:, c, h0 + tt * 512:h0 + (tt + 1) * 512],
                                           start=(c == 0), stop=(c == KC - 1))
                    return ins
                ring_gu.release()
                mm_g = tgt("mm")
                W("scalar", "mm", mm_g)
                Wg("scalar", dn_actb_gate.get(i))
                @EM("scalar", "sc")
                def _(e, i=i, p=p):
                    return e.activation(actb3[:, i, :], p["full"], AF.Copy)
                p["gate"] = ("sc", tgt("sc"))
                def mksilu(e_=e_, i=i):
                    def em():
                        @EM("scalar", "sc")
                        def _(e):
                            return e.activation(actb3[:, i, :], actb3[:, i, :], AF.Silu)
                        sc_silu_of[(e_, i)] = tgt("sc")
                    return em
                silu_pend.append(mksilu())

            def emit_cwrep(e_):
                buf = e_ % 2
                row = cwrow if buf == 0 else cwrow_b
                one = ones32 if buf == 0 else ones64
                p = pool_acquire()
                WT("tensor", cw_state["tok"].pop(e_))
                @EM("tensor", "mm")
                def _(e, h0=h0, p=p, row=row, one=one):
                    for tt in range(2):
                        ins = e.matmul(p["half"][tt], one,
                                       row[:, h0 + tt * 512:h0 + (tt + 1) * 512],
                                       start=True, stop=True)
                    return ins
                cw_state["mm"][buf] = tgt("mm")
                W("vector", "mm", cw_state["mm"][buf])
                @EM("vector", "ve")
                def _(e, p=p):
                    return e.tensor_copy(cwrep, p["full"])
                p["gate"] = ("ve", tgt("ve"))

            def emit_u(e_, i):
                uv, uw = ring_gu.acquire(wu_d[e_, i])
                p = pool_acquire()
                WT("tensor", uv)
                @EM("tensor", "mm")
                def _(e, uw=uw, h0=h0, p=p):
                    w3 = v3(uw, KC, 128)
                    for c in range(KC):
                        for tt in range(2):
                            ins = e.matmul(p["half"][tt], w3[:, c, :],
                                           xnb3[:, c, h0 + tt * 512:h0 + (tt + 1) * 512],
                                           start=(c == 0), stop=(c == KC - 1))
                    return ins
                ring_gu.release()
                mm_u = tgt("mm")
                if i == 0:
                    for j in range(ICH):
                        ring_d.acquire(wd_d[e_, j])
                W("vector", "mm", mm_u)
                W("vector", "sc", sc_silu_of[(e_, i)])
                if e_ < 16:
                    @EM("vector", "ve")
                    def _(e, p=p):
                        return e.tensor_tensor(us_full, p["full"], cwrep, ALU.mult)
                    p["gate"] = ("ve", tgt("ve"))
                    @EM("vector", "ve")
                    def _(e, i=i):
                        return e.tensor_tensor(actb3[:, i, :], actb3[:, i, :],
                                               us_full, ALU.mult)
                else:
                    @EM("vector", "ve")
                    def _(e, i=i, p=p):
                        return e.tensor_tensor(actb3[:, i, :], actb3[:, i, :],
                                               p["full"], ALU.mult)
                    p["gate"] = ("ve", tgt("ve"))
                ve_act_of[(e_, i)] = tgt("ve")
                uq_emitted[e_] = uq_emitted.get(e_, 0) + 1
                if uq_emitted[e_] == ICH and e_ not in down_queued:
                    down_queued.add(e_)
                    for hc in range(KC):
                        uq_ready.append(("down", e_, hc))

            def emit_down(e_, hc):
                if hc == 0:
                    lane_max = {}
                    for i in range(ICH):
                        l, v = ring_d.wqt[(h * NE + e_) * ICH + i]
                        lane_max[l] = max(lane_max.get(l, 0), v)
                    WT("tensor", list(lane_max.items()))
                    W("tensor", "ve", max(ve_act_of[(e_, i)] for i in range(ICH)))
                p = pool_acquire()
                @EM("tensor", "mm")
                def _(e, e_=e_, hc=hc, p=p):
                    for i in range(ICH):
                        w3 = v3(rdw[i], KC, 128)
                        for tt in range(2):
                            ins = e.matmul(p["half"][tt], w3[:, hc, :],
                                           actb3[:, i, tt * 512:(tt + 1) * 512],
                                           start=(i == 0), stop=(i == ICH - 1))
                    return ins
                if hc == KC - 1:
                    for _ in range(ICH):
                        ring_d.release()
                    for i in range(ICH):
                        dn_actb_gate[i] = ("mm", tgt("mm"))
                mm_dn = tgt("mm")
                W("vector", "mm", mm_dn)
                if hc == 0 and e_ == 0:
                    for t_ in tok_xf:
                        WT("vector", t_)
                @EM("vector", "ve")
                def _(e, hc=hc, p=p):
                    return e.tensor_tensor(outb3[:, hc, :], p["full"],
                                           outb3[:, hc, :], ALU.add)
                p["gate"] = ("ve", tgt("ve"))
                downs_done[e_] = downs_done.get(e_, 0) + 1

            dn_actb_gate = {}
            op_wts = {}

            def emit_op_q(g2, fc):
                if fc == 0:
                    toks, wts = [], []
                    for kc in (2 * g2, 2 * g2 + 1):
                        wv, wt = ring_ip.acquire(wop_d[kc])
                        toks.append(wv)
                        wts.append(wt)
                    op_wts[g2] = wts
                    for tok in toks:
                        WT("tensor", tok)
                    W("tensor", "ve", ve_y3[2 * g2 + 1])
                wts = op_wts[g2]
                p = pool_acquire()
                @EM("tensor", "mm")
                def _(e, g2=g2, fc=fc, wts=wts, p=p):
                    for tt in range(2):
                        for j in range(2):
                            w3 = v3(wts[j], KC, 128)
                            ins = e.matmul(p["half"][tt], w3[:, fc, :],
                                           ybf3[:, (2 * g2 + j) % 3,
                                                tt * 512:(tt + 1) * 512],
                                           start=(j == 0), stop=(j == 1))
                    return ins
                mm_opt = tgt("mm")
                W("vector", "mm", mm_opt)
                if g2 == 0 and fc == 0:
                    for t_ in tok_xf:
                        WT("vector", t_)
                @EM("vector", "ve")
                def _(e, fc=fc, p=p):
                    return e.tensor_tensor(outb3[:, fc, :], p["full"],
                                           outb3[:, fc, :], ALU.add)
                p["gate"] = ("ve", tgt("ve"))
                if fc == KC - 1:
                    ring_ip.release()
                    ring_ip.release()
                    mm_op_of_slot[(2 * g2) % 3] = tgt("mm")
                    mm_op_of_slot[(2 * g2 + 1) % 3] = tgt("mm")

            def pop_quanta(maxq):
                npop = 0
                while npop < maxq:
                    if uq_ready:
                        qd = uq_ready.pop(0)
                    elif gq_stage:
                        e_ = gq_stage[0][1]
                        if e_ >= 1 and downs_done.get(e_ - 1, 0) < KC:
                            return npop
                        qd = gq_stage.pop(0)
                    else:
                        return npop
                    if qd[0] == "g":
                        emit_g(qd[1], qd[2])
                    elif qd[0] == "cwrep":
                        emit_cwrep(qd[1])
                        continue
                    elif qd[0] == "u":
                        emit_u(qd[1], qd[2])
                    elif qd[0] == "op":
                        emit_op_q(qd[1], qd[2])
                    else:
                        emit_down(qd[1], qd[2])
                    npop += 1
                return npop

            def boundary():
                units = []
                for em in silu_pend:
                    em()
                silu_pend.clear()
                for key in list(sc_silu_of.keys()):
                    if key not in uq_done_mark:
                        uq_done_mark.add(key)
                        units.append(key)
                units.sort(key=lambda x: (x[0], x[1]))
                for (e_, i) in units:
                    if e_ < 16 and e_ not in cw_queued:
                        cw_queued.add(e_)
                        uq_ready.append(("cwrep", e_))
                    uq_ready.append(("u", e_, i))

            uq_done_mark = set()
            cw_queued = set()

            ve_y3 = {}
            ve_conv_prev = 0
            ve_nloop_end = 0
            ve_b = 0
            ve_ym = 0
            ve_scan = 0
            sc_ev = {0: None, 1: None}
            mm_op_of_slot = {}
            for k in range(DC):
                while any(q[0] == "op" for q in uq_ready):
                    pop_quanta(1)
                wv, wt = ring_ip.acquire(wip_d[k])
                p = pool_acquire()
                WT("tensor", wv)
                @EM("tensor", "mm")
                def _(e, wt=wt, h0=h0, p=p):
                    w3 = v3(wt, KC, 128)
                    for c in range(KC):
                        for tt in range(2):
                            ins = e.matmul(p["half"][tt], w3[:, c, :],
                                           xnb3[:, c, h0 + tt * 512:h0 + (tt + 1) * 512],
                                           start=(c == 0), stop=(c == KC - 1))
                    return ins
                ring_ip.release()
                mm_xp = tgt("mm")
                W("scalar", "mm", mm_xp)
                if k >= 1:
                    W("scalar", "ve", ve_conv_prev)
                elif h == 1:
                    W("scalar", "ve", ve_half_conv_last)
                @EM("scalar", "sc")
                def _(e, p=p):
                    return e.activation(xpp[:, 3:3 + TH], p["full"], AF.Copy)
                p["gate"] = ("sc", tgt("sc"))
                sc_xp2 = tgt("sc")
                wv, wt = ring_ip.acquire(wip_d[DC + k])
                p = pool_acquire()
                WT("tensor", wv)
                @EM("tensor", "mm")
                def _(e, wt=wt, h0=h0, p=p):
                    w3 = v3(wt, KC, 128)
                    for c in range(KC):
                        for tt in range(2):
                            ins = e.matmul(p["half"][tt], w3[:, c, :],
                                           xnb3[:, c, h0 + tt * 512:h0 + (tt + 1) * 512],
                                           start=(c == 0), stop=(c == KC - 1))
                    return ins
                ring_ip.release()
                mm_res = tgt("mm")
                W("scalar", "mm", mm_res)
                if k >= 1:
                    W("scalar", "ve", ve_y3.get(k - 1, 0))
                elif h == 1:
                    W("scalar", "ve", ve_half_y3_last)
                @EM("scalar", "sc")
                def _(e, p=p):
                    return e.activation(res_s, p["full"], AF.Copy)
                p["gate"] = ("sc", tgt("sc"))
                WT("tensor", tok_wdt[k % 2])
                p = pool_acquire()
                @EM("tensor", "mm")
                def _(e, k=k, h0=h0, p=p):
                    for tt in range(2):
                        ins = e.matmul(p["half"][tt], wdtb[k % 2],
                                       xdbl_s[0:64, h0 + tt * 512:h0 + (tt + 1) * 512],
                                       start=True, stop=True)
                    return ins
                mm_dt = tgt("mm")
                if k + 1 < DC:
                    tok_wdt[(k + 1) % 2] = HQ.dma(
                        wdtb[(k + 1) % 2], wdt_d[:, (k + 1) * 128:(k + 2) * 128],
                        gates=[("mm", mm_dt)])
                W("scalar", "mm", mm_dt)
                if k >= 1:
                    W("scalar", "ve", ve_nloop_end)
                elif h == 1:
                    W("scalar", "ve", ve_half_nloop_last)
                @EM("scalar", "sc")
                def _(e, p=p):
                    return e.activation(btile, p["full"], AF.Copy)
                p["gate"] = ("sc", tgt("sc"))
                sc_dpre = tgt("sc")
                pop_quanta(3)

                W("vector", "sc", sc_xp2)
                if h == 1:
                    VE1(lambda e, k=k: e.tensor_copy(xpp[:, 0:3],
                                                     ccar[:, k * 3:k * 3 + 3]))
                @EM("vector", "ve")
                def _(e, k=k):
                    e.tensor_scalar_mul(hh, xpp[:, 0:TH], wcv_s[:, k * 4:k * 4 + 1])
                    e.scalar_tensor_tensor(hh, xpp[:, 1:1 + TH],
                                           wcv_s[:, k * 4 + 1:k * 4 + 2],
                                           hh, ALU.mult, ALU.add)
                    e.scalar_tensor_tensor(hh, xpp[:, 2:2 + TH],
                                           wcv_s[:, k * 4 + 2:k * 4 + 3],
                                           hh, ALU.mult, ALU.add)
                    return e.scalar_tensor_tensor(hh, xpp[:, 3:3 + TH],
                                                  wcv_s[:, k * 4 + 3:k * 4 + 4],
                                                  hh, ALU.mult, ALU.add)
                if h == 0:
                    @EM("vector", "ve")
                    def _(e, k=k):
                        return e.tensor_copy(ccar[:, k * 3:k * 3 + 3], xpp[:, TH:TH + 3])
                ve_conv_prev = tgt("ve")
                W("vector", "sc", sc_dpre)
                @EM("vector", "ve")
                def _(e, k=k):
                    return e.tensor_scalar(btile, btile, cdb_s[:, k:k + 1], 80.0,
                                           ALU.add, ALU.min)
                ve_clamp = tgt("ve")

                W("scalar", "ve", ve_conv_prev)
                @EM("scalar", "sc")
                def _(e, k=k):
                    return e.activation(hh, hh, AF.Silu, bias=cvb_s[:, k:k + 1])
                @EM("scalar", "sc")
                def _(e):
                    return e.activation(res_s[:, :], res_s[:, :], AF.Silu)
                boundary()
                sc_hres = tgt("sc")
                W("scalar", "ve", ve_clamp)
                @EM("scalar", "sc")
                def _(e):
                    return e.activation(btile, btile, AF.Exp)
                @EM("scalar", "sc")
                def _(e):
                    return e.activation(delta, btile, AF.Ln, bias=c1_s[:, 0:1])
                sc_delta = tgt("sc")
                W("vector", "sc", sc_delta)
                @EM("vector", "ve")
                def _(e):
                    return e.tensor_tensor(dh, delta, hh, ALU.mult)
                ve_dh = tgt("ve")

                if h == 0 and k == 0:
                    W("tensor", "ve", ve_xdblb)
                    WT("tensor", tok_selb)
                for n in range(NST):
                    Wg("tensor", bankgate6[0])
                    @EM("tensor", "mm")
                    def _(e, n=n, h0=h0):
                        for q in range(2):
                            ins = e.matmul(bankB(2 + q), selb3[:, n, :],
                                           xdblb[0:32, h0 + q * 512:h0 + (q + 1) * 512],
                                           start=True, stop=True)
                        return ins
                    mm_B = tgt("mm")
                    Wg("tensor", pC["gate"])
                    @EM("tensor", "mm")
                    def _(e, n=n, h0=h0):
                        for q in range(2):
                            ins = e.matmul(pC["half"][q], selb3[:, NST + n, :],
                                           xdblb[0:32, h0 + q * 512:h0 + (q + 1) * 512],
                                           start=True, stop=True)
                        return ins
                    mm_C = tgt("mm")
                    if n > 0:
                        W("scalar", "ve", ve_scan)
                    else:
                        W("scalar", "ve", ve_clamp)
                    @EM("scalar", "sc")
                    def _(e, k=k, n=n):
                        return e.activation(atile, delta, AF.Exp,
                                            scale=a_s[:, k * NST + n:k * NST + n + 1])
                    sc_a = tgt("sc")
                    W("vector", "mm", mm_B)
                    if n == 0:
                        W("vector", "ve", ve_dh)
                    @EM("vector", "ve")
                    def _(e):
                        return e.tensor_tensor(btile, dh, pB[:, 1024:2048], ALU.mult)
                    ve_b = tgt("ve")
                    bankgate6[0] = ("ve", ve_b)
                    pop_quanta(1)
                    W("vector", "sc", sc_a)
                    if h == 0:
                        @EM("vector", "ve")
                        def _(e):
                            return e.tensor_tensor_scan(btile, atile, btile, 0.0,
                                                        ALU.mult, ALU.add)
                        ve_scan = tgt("ve")
                        W("vector", "ve")
                        @EM("vector", "ve")
                        def _(e, k=k, n=n):
                            return e.tensor_copy(state_s[:, k * NST + n:k * NST + n + 1],
                                                 btile[:, TH - 1:TH])
                    else:
                        @EM("vector", "ve")
                        def _(e, k=k, n=n):
                            return e.tensor_tensor_scan(
                                btile, atile, btile,
                                state_s[:, k * NST + n:k * NST + n + 1],
                                ALU.mult, ALU.add)
                        ve_scan = tgt("ve")
                    W("vector", "mm", mm_C)
                    if n == 0:
                        @EM("vector", "ve")
                        def _(e):
                            return e.tensor_tensor(yacc, btile, pC["full"], ALU.mult)
                        pC["gate"] = ("ve", tgt("ve"))
                    else:
                        @EM("vector", "ve")
                        def _(e):
                            return e.tensor_tensor(btile, btile, pC["full"], ALU.mult)
                        pC["gate"] = ("ve", tgt("ve"))
                        @EM("vector", "ve")
                        def _(e):
                            return e.tensor_tensor(yacc, yacc, btile, ALU.add)
                    ve_ym = tgt("ve")
                    pop_quanta(1)
                ve_nloop_end = tgt("ve")
                W("vector", "sc", sc_hres)
                W("vector", "mm", mm_op_of_slot.get(k % 3, 0))
                @EM("vector", "ve")
                def _(e, k=k):
                    e.scalar_tensor_tensor(yacc, hh, cdp_s[:, k:k + 1], yacc,
                                           ALU.mult, ALU.add)
                    return e.tensor_tensor(ybf3[:, k % 3, :], yacc, res_s[:, :], ALU.mult)
                ve_y3[k] = tgt("ve")
                ve_nloop_end = ve_y3[k]
                if k % 2 == 1:
                    for fc in range(KC):
                        uq_ready.append(("op", k // 2, fc))
            guard = 0
            while gq_stage or uq_ready or silu_pend:
                guard += 1
                assert guard < 200, "feeder stuck"
                if not uq_ready and (silu_pend or gq_stage):
                    pop_quanta(50)
                    boundary()
                else:
                    pop_quanta(50)
            ve_all = tgt("ve")
            for fc in range(KC):
                t_ = STQ.dma(of_d[fc][:, h0:h0 + TH], outb3[:, fc, :],
                             gates=[("ve", ve_all)] if fc == 0 else [])
                if h == 0:
                    h0_store_toks.append(t_)
            ve_half_store = ve_all
            if h == 0:
                ve_half_conv_last = ve_conv_prev
                ve_half_y3_last = ve_y3[DC - 1]
                ve_half_nloop_last = ve_nloop_end
                mm_dt_h0 = mm_dt
            tap(f"outh{h}", outb3)
            snap(f"H{h}")
        for (l, v) in STQ.barrier():
            W("sync", l, v)

        if STOP is not None:
            cut = snapshots[STOP]
            for e in ops:
                ops[e] = ops[e][:cut[e]]

        with nc.Block() as blk:
            @blk.tensor
            def _(e):
                for fn in ops["tensor"]:
                    fn(e)
            @blk.vector
            def _(e):
                for fn in ops["vector"]:
                    fn(e)
            @blk.scalar
            def _(e):
                for fn in ops["scalar"]:
                    fn(e)
            @blk.sync
            def _(e):
                for fn in ops["sync"]:
                    fn(e)
            @blk.gpsimd
            def _(e):
                for fn in ops["gpsimd"]:
                    fn(e)

    return nc


def _host_prep(inputs):
    f32 = np.float32
    ip = np.ascontiguousarray(inputs["L0_attn_in_proj_w"], dtype=f32)
    wip = np.ascontiguousarray(
        ip.reshape(KC, 128, 32, 128).transpose(2, 1, 0, 3))
    wxp = np.ascontiguousarray(
        inputs["L0_attn_x_proj_w"].reshape(DC, 128, 96).transpose(1, 0, 2).reshape(128, DC * 96),
        dtype=f32)
    wdt = np.ascontiguousarray(inputs["L0_attn_dt_proj_w"], dtype=f32)
    wop = np.ascontiguousarray(inputs["L0_attn_out_proj_w"].reshape(DC, 128, D), dtype=f32)
    wgt = np.ascontiguousarray(
        inputs["L0_ffn_gate_w"].T.reshape(KC, 128, 16).transpose(1, 0, 2).reshape(128, KC * 16),
        dtype=f32)
    cgb = np.ascontiguousarray(np.broadcast_to(inputs["L0_ffn_gate_bias"], (128, 16)), dtype=f32)
    ceps = np.ascontiguousarray(
        np.broadcast_to(np.arange(16, dtype=f32) * 1e-12, (128, 16)))
    wcv = np.ascontiguousarray(
        inputs["L0_attn_conv_w"].reshape(DC, 128, 4).transpose(1, 0, 2).reshape(128, DC * 4),
        dtype=f32)
    cvb = np.ascontiguousarray(inputs["L0_attn_conv_b"].reshape(DC, 128).T, dtype=f32)
    cdb = np.ascontiguousarray(inputs["L0_attn_dt_proj_b"].reshape(DC, 128).T, dtype=f32)
    cdp = np.ascontiguousarray(inputs["L0_attn_D_param"].reshape(DC, 128).T, dtype=f32)
    cal = np.ascontiguousarray(
        inputs["L0_attn_A_log"].reshape(DC, 128, NST).transpose(1, 0, 2).reshape(128, DC * NST),
        dtype=f32)

    def expert_lhsT(w):
        return np.ascontiguousarray(
            w.reshape(w.shape[0], ICH, 128, KC, 128).transpose(0, 1, 4, 3, 2))

    gp = np.asarray(inputs["L0_ffn_gate_proj"], dtype=f32)
    up = np.asarray(inputs["L0_ffn_up_proj"], dtype=f32)
    sg = np.asarray(inputs["L0_ffn_shared_gate_w"], dtype=f32).reshape(2, 1408, D)
    su = np.asarray(inputs["L0_ffn_shared_up_w"], dtype=f32).reshape(2, 1408, D)
    wg3 = expert_lhsT(np.concatenate([gp, sg], axis=0))
    wu3 = expert_lhsT(np.concatenate([up, su], axis=0))
    dp = np.asarray(inputs["L0_ffn_down_proj"], dtype=f32)
    sd = np.asarray(inputs["L0_ffn_shared_down_w"], dtype=f32)
    dall = np.concatenate([dp, sd.reshape(D, 2, 1408).transpose(1, 0, 2)], axis=0)
    wd3 = np.ascontiguousarray(
        dall.reshape(NE, KC, 128, ICH, 128).transpose(0, 3, 4, 1, 2))
    i128f = np.eye(128, dtype=f32)
    selb = np.zeros((32, 32, 128), dtype=f32)
    for n in range(32):
        selb[n, n, :] = 1.0
    selb = selb.reshape(32, 32 * 128)
    return dict(wip=wip, wxp=wxp, wdt=wdt, wop=wop, wgt=wgt, cgb=cgb,
                ceps=ceps, wcv=wcv, cvb=cvb, cdb=cdb, cdp=cdp, cal=cal,
                wg3=wg3, wu3=wu3, wd3=wd3, i128f=i128f, selb=selb)


def kernel(**inputs):
    x = np.asarray(inputs["L0_attn_x"], dtype=np.float32)
    shared = _host_prep(inputs)
    nc = _build()
    in_maps = []
    for b in range(8):
        xb = np.ascontiguousarray(x[b])
        m = dict(shared)
        m["xt"] = np.ascontiguousarray(xb.reshape(NTILE, 128, D))
        m["xf"] = np.ascontiguousarray(xb.T.reshape(KC, 128, T))
        in_maps.append(m)
    res = run_bass_kernel_spmd(nc, in_maps, core_ids=list(range(8)))
    out = np.empty((8, T, D), dtype=np.float32)
    for b in range(8):
        of = np.asarray(res.results[b]["of"])
        out[b] = of.reshape(D, T).T
    return out
